# version 12
# speedup vs baseline: 1.8519x; 1.0524x over previous
import numpy as np
from contextlib import ExitStack

import concourse.bass as bass
import concourse.mybir as mybir
from concourse.bass_utils import run_bass_kernel_spmd

F32 = mybir.dt.float32
BF16 = mybir.dt.bfloat16
I32 = mybir.dt.int32
I8 = mybir.dt.int8
AF = mybir.ActivationFunctionType
ALU = mybir.AluOpType
AX = mybir.AxisListType

T = 4096
D = 1024
NT = T // 128
ND = D // 128

LN_EPS = 1e-5


class Ctr:
    def __init__(self):
        self.v = 0

    def add(self, n):
        self.v += n
        return self.v


class Prog:

    ENGS = ("tensor", "vector", "scalar", "sync", "gpsimd")

    def __init__(self, nc, ctx):
        self.nc = nc
        self.ctx = ctx
        self.streams = {e: [] for e in self.ENGS}
        self.sems = {}
        self.ctrs = {}
        self.pfx = ""

    def sem(self, name):
        if name not in self.sems:
            self.sems[name] = self.ctx.enter_context(self.nc.semaphore(name))
            self.ctrs[name] = Ctr()
        return self.sems[name]

    def emit(self, eng, fn):
        self.streams[eng].append(fn)

    def wait(self, eng, sem_name, val=None):
        s = self.sem(sem_name)
        v = self.ctrs[sem_name].v if val is None else val
        if v > 0:
            self.emit(eng, lambda e, s=s, v=v: e.wait_ge(s, v))
        return v

    def inc_after(self, eng, sem_name, n, fn):
        s = self.sem(sem_name)
        self.ctrs[sem_name].add(n)
        self.emit(eng, lambda e, s=s, n=n, fn=fn: fn(e).then_inc(s, n))

    def now(self, sem_name):
        self.sem(sem_name)
        return self.ctrs[sem_name].v

    def barrier(self):
        s = self.sem("bar")
        self.ctrs["bar"].add(len(self.ENGS))
        target = self.ctrs["bar"].v
        for eng in self.ENGS:
            self.emit(eng, lambda e, s=s: e.sem_inc(s, 1))
        for eng in self.ENGS:
            self.emit(eng, lambda e, s=s, v=target: e.wait_ge(s, v))

    def run(self):
        block = self.ctx.enter_context(self.nc.Block())
        for eng in self.ENGS:
            fns = self.streams[eng]
            getattr(block, eng)(lambda e, fns=fns: [f(e) for f in fns])

    def t(self, fn):
        self.emit("tensor", fn)

    def v(self, fn):
        self.emit("vector", fn)

    def s(self, fn):
        self.emit("scalar", fn)


def emit_ln_tile(p, s32_ap, out_ap, stats, sq_scr=None):
    ssq, ssum, mean, var, rstd = (
        stats["ssq"], stats["ssum"], stats["mean"], stats["var"], stats["rstd"])
    if sq_scr is None:
        sq_scr = stats["sq_scr"][:, :]
    p.wait("scalar", "ve")
    p.inc_after("scalar", "se", 1,
                lambda e: e.activation(sq_scr, s32_ap, AF.Square, accum_out=ssq[:, 0:1]))
    p.wait("vector", "ve")
    p.inc_after("vector", "ve", 1,
                lambda e: e.reduce_sum(ssum[:, 0:1], s32_ap, axis=AX.X))
    p.wait("vector", "ve")
    p.wait("vector", "se")
    p.inc_after("vector", "ve", 1,
                lambda e: e.tensor_scalar_mul(mean[:, 0:1], ssum[:, 0:1], 1.0 / D))
    p.wait("vector", "ve")
    p.inc_after("vector", "ve", 1,
                lambda e: e.tensor_scalar_mul(var[:, 0:1], ssq[:, 0:1], 1.0 / D))
    p.wait("vector", "ve")
    p.inc_after("vector", "ve", 1,
                lambda e: e.tensor_mul(ssum[:, 0:1], mean[:, 0:1], mean[:, 0:1]))
    p.wait("vector", "ve")
    p.inc_after("vector", "ve", 1,
                lambda e: e.tensor_sub(var[:, 0:1], var[:, 0:1], ssum[:, 0:1]))
    p.wait("scalar", "ve")
    p.inc_after("scalar", "se", 1,
                lambda e: e.activation(rstd[:, 0:1], var[:, 0:1], AF.Ln,
                                       bias=stats["eps"][:, 0:1]))
    p.wait("scalar", "se")
    p.inc_after("scalar", "se", 1,
                lambda e: e.activation(rstd[:, 0:1], rstd[:, 0:1], AF.Exp, scale=-0.5))
    p.wait("vector", "se")
    p.inc_after("vector", "ve", 1,
                lambda e: e.tensor_scalar(out_ap, s32_ap, mean[:, 0:1], rstd[:, 0:1],
                                          ALU.subtract, ALU.mult))


def alloc_stats(sbt, tag=""):
    return dict(
        ssq=sbt("ssq" + tag, [128, 1], F32),
        ssum=sbt("ssum" + tag, [128, 1], F32),
        mean=sbt("mean" + tag, [128, 1], F32),
        var=sbt("var" + tag, [128, 1], F32),
        rstd=sbt("rstd" + tag, [128, 1], F32),
        eps=sbt("eps" + tag, [128, 1], F32),
    )



def build_ffn_int8(H=3072, T_=None, p=None, dram=None):
    T = T_ or globals()["T"]
    NH = H // 128
    TS = 256
    NS = T // TS
    TPS = TS // 128

    standalone = p is None
    if standalone:
        _nc = bass.Bass("TRN2", target_bir_lowering=False)
        _octx = ExitStack()
        _octx.__enter__()
        p = Prog(_nc, _octx)
        p.pfx = ""
        dram = lambda name, shape, dt, kind: _nc.dram_tensor(name, shape, dt, kind=kind).ap()
    nc = p.nc
    x = dram("x", [T, D], F32, "ExternalInput")
    wgT = dram("wgT", [D, H], I8, "ExternalInput")
    wuT = dram("wuT", [D, H], I8, "ExternalInput")
    wdT = dram("wdT", [H, D], I8, "ExternalInput")
    ident = dram("c_ident", [128, 128], F32, "ExternalInput")
    yo = dram("y", [T, D], F32, "ExternalOutput")

    with ExitStack() as ctx:
        sbt = lambda n, *a: ctx.enter_context(nc.sbuf_tensor(p.pfx + n, *a))
        pst = lambda n, *a: ctx.enter_context(nc.psum_tensor(p.pfx + n, *a))

        IDN = sbt("IDN", [128, 128], BF16)
        WG = sbt("WG", [128, NH * ND * 128], BF16)
        WU = sbt("WU", [128, NH * ND * 128], BF16)
        WD = sbt("WD", [128, NH * D], BF16)
        WSTG = [sbt("WSTG0", [128, 3072], I8), sbt("WSTG1", [128, 3072], I8)]
        XSB = sbt("XSB", [128, TPS * D], BF16)
        XS32 = sbt("XS32", [128, TPS * D], F32)
        XF = [sbt(f"XF{i}", [128, ND * TS], BF16) for i in range(2)]
        GS = sbt("GS", [128, TS], BF16)
        HS = sbt("HS", [128, TS], BF16)
        S32 = sbt("S32", [128, TPS * D], F32)
        OUT = sbt("OUT", [128, TPS * D], F32)
        stats = alloc_stats(sbt)

        ps_tr = pst("ps_tr", [128, ND * 128], BF16)
        ps_gb = [pst(f"ps_g{i}", [128, TS], F32) for i in range(2)]
        ps_u = pst("ps_u", [128, TS], F32)
        ps_d = [pst(f"ps_d{i}", [128, D], F32) for i in range(TPS)]

        p.inc_after("vector", "ve", 1, lambda e: e.memset(stats["eps"][:, 0:1], LN_EPS))
        p.inc_after("gpsimd", "gld", 16, lambda e: e.dma_start(out=IDN[:, :], in_=ident))

        wchunks = []
        for wT, dest, nrow_chunks, ncols, kind in [
                (wgT, WG, ND, H, "g"), (wuT, WU, ND, H, "u"), (wdT, WD, NH, D, "d")]:
            srcv = wT.rearrange("(rc p) f -> rc p f", p=128)
            for rc in range(nrow_chunks):
                if kind == "d":
                    out_ap = dest[:, rc * ncols:(rc + 1) * ncols]
                else:
                    out_ap = dest[:, :].rearrange(
                        "p (hc nd f) -> p hc nd f", nd=ND, f=128)[:, :, rc, :]
                wchunks.append((srcv[rc], out_ap, ncols))
        se_w0 = p.now("se")
        wbase = [p.now("wld0"), p.now("wld1")]
        nck = len(wchunks)
        for ci, (src_ap, out_ap, ncols) in enumerate(wchunks):
            stg = WSTG[ci % 2]
            if ci >= 2:
                p.wait("sync", "se", se_w0 + ci - 1)
            p.inc_after("sync", "wld" + str(ci % 2), 16,
                        lambda e, src_ap=src_ap, stg=stg, ncols=ncols: e.dma_start(
                            out=stg[:, 0:ncols], in_=src_ap))
            p.wait("scalar", "wld" + str(ci % 2), wbase[ci % 2] + 16 * (ci // 2 + 1))
            p.inc_after("scalar", "se", 1,
                        lambda e, stg=stg, out_ap=out_ap, ncols=ncols:
                        e.activation(out_ap, stg[:, 0:ncols], AF.Copy, scale=1.0 / 127.0))
        se_wg = se_w0 + ND
        se_wu = se_w0 + 2 * ND
        se_w = p.now("se")

        x_t = x.rearrange("(nt p) d -> nt p d", p=128)
        yo_t = yo.rearrange("(nt p) d -> nt p d", p=128)

        for s in range(NS):
            xf = XF[s % 2]
            if s > 0:
                p.wait("gpsimd", "te")
                p.wait("sync", "ve")
            for tt in range(TPS):
                nt = s * TPS + tt
                p.inc_after("gpsimd", "gld", 16,
                            lambda e, nt=nt, tt=tt: e.dma_start(
                                out=XSB[:, tt * D:(tt + 1) * D], in_=x_t[nt]))
                p.inc_after("sync", "ld", 16,
                            lambda e, nt=nt, tt=tt: e.dma_start(
                                out=XS32[:, tt * D:(tt + 1) * D], in_=x_t[nt]))
            ld_slab = p.now("ld")
            gld_slab = p.now("gld")

            p.wait("tensor", "gld", gld_slab)
            xf_view = xf[:, :].rearrange("p (nd ts) -> p nd ts", ts=TS)
            for tt in range(TPS):
                p.wait("tensor", "ve")
                for dc in range(ND):
                    fn = lambda e, tt=tt, dc=dc: e.transpose(
                        ps_tr[:, dc * 128:(dc + 1) * 128],
                        XSB[:, tt * D + dc * 128: tt * D + (dc + 1) * 128], IDN[:, :])
                    if dc == ND - 1:
                        p.inc_after("tensor", "te", 1, fn)
                    else:
                        p.t(fn)
                p.wait("vector", "te")
                p.inc_after("vector", "ve", 1,
                            lambda e, xf_view=xf_view, tt=tt: e.tensor_copy(
                                xf_view[:, :, tt * 128:(tt + 1) * 128],
                                ps_tr[:, :].rearrange("p (nd f) -> p nd f", f=128)))
            ve_xf = p.now("ve")

            p.wait("tensor", "ve", ve_xf)
            se_silu = {}
            ve_h = {}
            te_g = {}
            te_u = {}

            def emit_down(hc):
                p.wait("tensor", "ve", ve_h[hc])
                if s == 0 and hc == 0:
                    p.wait("tensor", "se", se_w)
                for i, (tt, half) in enumerate([(a, b) for a in range(TPS) for b in range(2)]):
                    fn = lambda e, hc=hc, tt=tt, half=half: e.matmul(
                        ps_d[tt][:, half * 512:(half + 1) * 512],
                        HS[:, tt * 128:(tt + 1) * 128],
                        WD[:, hc * D + half * 512: hc * D + half * 512 + 512],
                        start=(hc == 0), stop=(hc == NH - 1))
                    if i == 2 * TPS - 1:
                        p.inc_after("tensor", "te", 1, fn)
                    else:
                        p.t(fn)

            for hc in range(NH):
                ps_g = ps_gb[hc % 2]
                if hc >= 2:
                    p.wait("tensor", "se", se_silu[hc - 2])
                if s == 0 and hc == 0:
                    p.wait("tensor", "se", se_wg)
                for dc in range(ND):
                    fn = lambda e, hc=hc, dc=dc, ps_g=ps_g, xf=xf: e.matmul(
                        ps_g[:, :], WG[:, (hc * ND + dc) * 128:(hc * ND + dc + 1) * 128],
                        xf[:, dc * TS:(dc + 1) * TS],
                        start=(dc == 0), stop=(dc == ND - 1))
                    if dc == ND - 1:
                        p.inc_after("tensor", "te", 1, fn)
                    else:
                        p.t(fn)
                te_g[hc] = p.now("te")
                if hc >= 1:
                    emit_down(hc - 1)
                if hc >= 1:
                    p.wait("tensor", "ve", ve_h[hc - 1])
                if s == 0 and hc == 0:
                    p.wait("tensor", "se", se_wu)
                for dc in range(ND):
                    fn = lambda e, hc=hc, dc=dc, xf=xf: e.matmul(
                        ps_u[:, :], WU[:, (hc * ND + dc) * 128:(hc * ND + dc + 1) * 128],
                        xf[:, dc * TS:(dc + 1) * TS],
                        start=(dc == 0), stop=(dc == ND - 1))
                    if dc == ND - 1:
                        p.inc_after("tensor", "te", 1, fn)
                    else:
                        p.t(fn)
                te_u[hc] = p.now("te")
                p.wait("scalar", "te", te_g[hc])
                if hc >= 1:
                    p.wait("scalar", "ve", ve_h[hc - 1])
                p.inc_after("scalar", "se", 1,
                            lambda e, ps_g=ps_g: e.activation(GS[:, :], ps_g[:, :], AF.Silu))
                se_silu[hc] = p.now("se")
                p.wait("vector", "te", te_u[hc])
                p.wait("vector", "se", se_silu[hc])
                p.inc_after("vector", "ve", 1,
                            lambda e: e.tensor_tensor(HS[:, :], GS[:, :], ps_u[:, :], ALU.mult))
                ve_h[hc] = p.now("ve")
            emit_down(NH - 1)
            te_down = p.now("te")

            for tt in range(TPS):
                nt = s * TPS + tt
                p.wait("vector", "te", te_down)
                p.wait("vector", "ld", ld_slab)
                p.inc_after("vector", "ve", 1,
                            lambda e, tt=tt: e.tensor_tensor(
                                S32[:, tt * D:(tt + 1) * D], XS32[:, tt * D:(tt + 1) * D],
                                ps_d[tt][:, :], ALU.add))
                p.wait("scalar", "st")
                p.wait("vector", "st")
                emit_ln_tile(p, S32[:, tt * D:(tt + 1) * D], OUT[:, tt * D:(tt + 1) * D], stats,
                             sq_scr=OUT[:, tt * D:(tt + 1) * D])
                ve_out = p.now("ve")
                p.wait("sync", "ve", ve_out)
                p.inc_after("sync", "st", 16,
                            lambda e, nt=nt, tt=tt: e.dma_start(
                                out=yo_t[nt], in_=OUT[:, tt * D:(tt + 1) * D]))

        p.wait("sync", "st")
        p.barrier()
    if standalone:
        p.run()
        _octx.__exit__(None, None, None)
    return nc




def build_attn(kind, T_=None, NHEADS=16, HG=4, p=None, dram=None):
    Tl = T_ or T
    NTl = Tl // 128
    HD = 64
    NG = NHEADS // HG
    GR = HG * HD
    GC = GR // 128
    NQS = Tl // 512
    F = 3 * D
    lora = kind == "l1"

    standalone = p is None
    if standalone:
        _nc = bass.Bass("TRN2", target_bir_lowering=False)
        _octx = ExitStack()
        _octx.__enter__()
        p = Prog(_nc, _octx)
        p.pfx = ""
        dram = lambda name, shape, dt, kind: _nc.dram_tensor(name, shape, dt, kind=kind).ap()
    nc = p.nc
    x = dram("x", [Tl, D], F32, "ExternalInput")
    wqkv = dram("wqkv", [D, F], F32, "ExternalInput")
    bqkv = dram("bqkv", [1, F], F32, "ExternalInput")
    bqkvc = dram("bqkvc", [128, F // 128], F32, "ExternalInput")
    wproj = dram("wproj", [D, D], F32, "ExternalInput")
    bproj = dram("bproj", [1, D], F32, "ExternalInput")
    if lora:
        aqkv = dram("aqkv", [D, 8], F32, "ExternalInput")
        bl_qkv = dram("bl_qkv", [8, F], F32, "ExternalInput")
        aproj = dram("aproj", [D, 8], F32, "ExternalInput")
        bl_proj = dram("bl_proj", [8, D], F32, "ExternalInput")
    else:
        ccos = dram("c_cosc", [128, 8], F32, "ExternalInput")
        csin = dram("c_sinc", [128, 8], F32, "ExternalInput")
        cperm = dram("c_perm", [128, 128], F32, "ExternalInput")
        cmask = dram("c_mask", [128, 128], F32, "ExternalInput")
    ident = dram("c_ident", [128, 128], F32, "ExternalInput")
    yo = dram("y", [Tl, D], F32, "ExternalOutput")

    x_t = x.rearrange("(nt p) d -> nt p d", p=128)
    yo_t = yo.rearrange("(nt p) d -> nt p d", p=128)

    with ExitStack() as ctx:
        sbt = lambda n, *a: ctx.enter_context(nc.sbuf_tensor(p.pfx + n, *a))
        pst = lambda n, *a: ctx.enter_context(nc.psum_tensor(p.pfx + n, *a))

        IDN = sbt("IDN", [128, 128], BF16)
        XF = sbt("XFa", [128, ND * Tl], BF16)
        OF = sbt("OFa", [128, ND * Tl], BF16)
        WQG = sbt("WQG", [128, ND * 2 * GR], BF16)
        KGW = sbt("KGW", [128, max(GC * Tl, ND * D)], BF16)
        KG = KGW
        WPROJ = KGW
        VT = sbt("VT", [128, NTl * HG * 65], BF16)
        QF = sbt("QF", [128, GC * 512], BF16)
        BQC = sbt("BQC", [128, F // 128], F32)
        BVB = sbt("BVB", [128, GR], BF16)
        PBJ = sbt("PBJ", [128, 1024], BF16)
        PB = [PBJ[:, 0:512], PBJ[:, 512:1024]]
        PBJ2 = sbt("PBJ2", [128, 1024], BF16)
        PB2 = [PBJ2[:, 0:512], PBJ2[:, 512:1024]]
        OS = sbt("OS", [128, 512], BF16)
        OUT = sbt("OUTa", [128, D], F32)
        ONE1 = sbt("ONE1", [1, 128], F32)
        SHARE = sbt("SHARE", [128, 4 * D], BF16)
        XSB = SHARE[:, 0:2 * D]
        S32 = SHARE[:, :].bitcast(F32)
        OAS = SHARE[:, 0:1024].bitcast(F32)
        OBS = SHARE[:, 1024:2048].bitcast(F32)
        QF2 = SHARE[:, 2048:3072]
        RSC2 = SHARE[:, 3072:4096].bitcast(F32)
        BPB = sbt("BPB", [128, D], BF16)
        BPR = sbt("BPR", [1, D], F32)
        DEN = BPR[0:1, 0:512]
        RECIP = BPR[0:1, 512:1024]
        BVR = BPR[0:1, 0:GR]
        stats = alloc_stats(sbt)
        if lora:
            AQ = sbt("AQ", [128, ND * 8], BF16)
            BLG = sbt("BLG", [8, 3 * GR], BF16)
            APJ = sbt("APJ", [128, ND * 8], BF16)
            BLP = sbt("BLP", [8, D], BF16)
            MIDB = sbt("MIDB", [8, Tl], BF16)
        else:
            COS = sbt("COSC", [128, 8], F32)
            SIN = sbt("SINC", [128, 8], F32)
            PERM = sbt("PERM", [128, 128], BF16)
            MASK = sbt("MASKT", [128, 128], BF16)
            RSC = PBJ[:, :].bitcast(F32)

        ps_a = pst("ps_a", [128, 512], F32)
        ps_rb = pst("ps_rb", [128, 512], F32)
        ps_tr_ctx = ExitStack()
        ps_tr = ps_tr_ctx.enter_context(nc.psum_tensor(p.pfx + "ps_tr", [128, ND * 128], BF16))

        def mm_chain(fns, extra=None):
            all_fns = fns + ([extra] if extra is not None else [])
            for i, fn in enumerate(all_fns):
                if i == len(all_fns) - 1:
                    p.inc_after("tensor", "te", 1, fn)
                else:
                    p.t(fn)
            return p.now("te")

        p.inc_after("vector", "ve", 1, lambda e: e.memset(stats["eps"][:, 0:1], LN_EPS))
        p.inc_after("vector", "ve", 1, lambda e: e.memset(ONE1[:, :], 1.0))
        p.inc_after("vector", "ve", 1, lambda e: e.memset(
            VT[:, :].rearrange("q (nt h c) -> q nt h c", h=HG, c=65)[:, :, :, 64:65], 1.0))

        p.inc_after("gpsimd", "gld", 16, lambda e: e.dma_start(out=IDN[:, :], in_=ident))
        p.inc_after("sync", "ld", 16, lambda e: e.dma_start(out=BQC[:, :], in_=bqkvc))
        if lora:
            p.inc_after("gpsimd", "gld", 16, lambda e: e.dma_start(
                out=AQ[:, :].rearrange("p (dc r) -> p dc r", r=8),
                in_=aqkv.rearrange("(dc p) r -> p dc r", p=128)))
            p.inc_after("gpsimd", "gld", 16, lambda e: e.dma_start(
                out=APJ[:, :].rearrange("p (dc r) -> p dc r", r=8),
                in_=aproj.rearrange("(dc p) r -> p dc r", p=128)))
            p.inc_after("gpsimd", "gld", 16, lambda e: e.dma_start(out=BLP[:, :], in_=bl_proj))
        else:
            p.inc_after("sync", "ld", 16, lambda e: e.dma_start(out=COS[:, :], in_=ccos))
            p.inc_after("sync", "ld", 16, lambda e: e.dma_start(out=SIN[:, :], in_=csin))
            p.inc_after("gpsimd", "gld", 16, lambda e: e.dma_start(out=PERM[:, :], in_=cperm))
            p.inc_after("gpsimd", "gld", 16, lambda e: e.dma_start(out=MASK[:, :], in_=cmask))

        te_nt = []
        for nt in range(NTl):
            buf = (nt % 2) * D
            par = f"gx{nt % 2}"
            if nt >= 2:
                p.wait("gpsimd", "te", te_nt[nt - 2])
            p.inc_after("gpsimd", par, 16,
                        lambda e, nt=nt, buf=buf: e.dma_start(out=XSB[:, buf:buf + D], in_=x_t[nt]))
            p.wait("tensor", par)
            if nt == 0:
                p.wait("tensor", "gld")
            p.wait("tensor", "ve")
            fns = [
                (lambda e, nt=nt, dc=dc, buf=buf: e.transpose(
                    ps_tr[:, dc * 128:(dc + 1) * 128],
                    XSB[:, buf + dc * 128: buf + (dc + 1) * 128], IDN[:, :]))
                for dc in range(ND)]
            te_nt.append(mm_chain(fns))
            p.wait("vector", "te")
            xf_view = XF[:, :].rearrange("q (nd t) -> q nd t", t=Tl)
            p.inc_after("vector", "ve", 1,
                        lambda e, nt=nt, xf_view=xf_view: e.tensor_copy(
                            xf_view[:, :, nt * 128:(nt + 1) * 128],
                            ps_tr[:, :].rearrange("q (nd f) -> q nd f", f=128)))
        ve_xf = p.now("ve")
        p.wait("tensor", "ve", ve_xf)
        ps_tr_ctx.close()
        ps_at_ctx = ExitStack()
        _pat = lambda n, s, d: ps_at_ctx.enter_context(nc.psum_tensor(p.pfx + n, s, d))
        ps_sa = [_pat(f"ps_sa{i}", [128, 512], F32) for i in range(2)]
        ps_sb = [_pat(f"ps_sb{i}", [128, 512], F32) for i in range(2)]
        ps_oa = _pat("ps_oa", [128, 512], F32)
        ps_ob = _pat("ps_ob", [128, 512], F32)

        if lora:
            for tc in range(Tl // 512):
                if tc > 0:
                    p.wait("tensor", "se")
                mm_chain([
                    (lambda e, tc=tc, dc=dc: e.matmul(
                        ps_a[0:8, :], AQ[:, dc * 8:(dc + 1) * 8],
                        XF[:, dc * Tl + tc * 512: dc * Tl + tc * 512 + 512],
                        start=(dc == 0), stop=(dc == ND - 1)))
                    for dc in range(ND)])
                p.wait("scalar", "te")
                p.inc_after("scalar", "se", 1,
                            lambda e, tc=tc: e.activation(
                                MIDB[0:8, tc * 512:(tc + 1) * 512], ps_a[0:8, :],
                                AF.Copy, scale=2.0))
        se_mid = p.now("se")
        p.wait("tensor", "se", se_mid)

        def emit_rope(dst_sl, col):
            tgt, sl = dst_sl
            p.wait("vector", "te")
            p.wait("tensor", "ve")
            p.inc_after("tensor", "te", 1,
                        lambda e, tgt=tgt, sl=sl: e.matmul(ps_rb[:, :], PERM[:, :], tgt[:, sl],
                                                           start=True, stop=True))
            p.wait("vector", "te")
            p.wait("vector", "ld")
            p.inc_after("vector", "ve", 1,
                        lambda e, tgt=tgt, sl=sl, col=col: e.tensor_scalar(
                            RSC[:, :], tgt[:, sl], COS[:, col:col + 1], None, ALU.mult))
            p.wait("vector", "ve")
            p.inc_after("vector", "ve", 1,
                        lambda e, tgt=tgt, sl=sl, col=col: e.scalar_tensor_tensor(
                            tgt[:, sl], ps_rb[:, :], SIN[:, col:col + 1], RSC[:, :],
                            ALU.mult, ALU.add))

        QFB = [QF, QF2]
        state = {"rb_free_ve": 0, "psov_free_ve": 0, "qprep_psa_ve": 0}
        pending = []
        qprep_ready = {}

        for g in range(NG):
            p.wait("gpsimd", "te")
            for si, base in ((0, D + g * GR), (1, 2 * D + g * GR)):
                p.inc_after("gpsimd", "gw", 16,
                            lambda e, si=si, base=base: e.dma_start(
                                out=WQG[:, :].rearrange("q (nd s c) -> q nd s c", s=2, c=GR)[:, :, si, :],
                                in_=wqkv.rearrange("(dc q) f -> q dc f", q=128)[:, :, base:base + GR]))
            if lora:
                for si, base in enumerate((g * GR, D + g * GR, 2 * D + g * GR)):
                    p.inc_after("gpsimd", "gw", 16,
                                lambda e, si=si, base=base: e.dma_start(
                                    out=BLG[0:8, si * GR:(si + 1) * GR],
                                    in_=bl_qkv[0:8, base:base + GR]))
            p.wait("sync", "ve")
            p.inc_after("sync", "ld", 16,
                        lambda e, g=g: e.dma_start(out=BVR[0:1, :],
                                                   in_=bqkv[0:1, 2 * D + g * GR: 2 * D + g * GR + GR]))
            ld_bv = p.now("ld")
            gld_w = p.now("gw")
            p.wait("tensor", "ld", ld_bv)
            p.wait("tensor", "gw", gld_w)
            p.wait("tensor", "ve")
            mm_chain([lambda e: e.matmul(ps_rb[:, 0:GR], ONE1[0:1, :], BVR[0:1, :],
                                         start=True, stop=True)])
            p.wait("vector", "te")
            p.inc_after("vector", "ve", 1,
                        lambda e: e.tensor_copy(BVB[:, :], ps_rb[:, 0:GR]))
            ve_bvb = p.now("ve")

            kidx = [(cc, tc) for cc in range(GC) for tc in range(Tl // 512)]
            kpsb = [ps_a, ps_rb]
            ktm = {}
            kvm = {}

            def emit_kevac(idx):
                cc, tc = kidx[idx]
                fglob = D + g * GR + cc * 128
                psk = kpsb[idx % 2]
                p.wait("vector", "te", ktm[idx])
                p.inc_after("vector", "ve", 1,
                            lambda e, cc=cc, tc=tc, fglob=fglob, psk=psk: e.tensor_scalar_add(
                                KG[:, cc * Tl + tc * 512: cc * Tl + tc * 512 + 512],
                                psk[:, :], BQC[:, fglob // 128: fglob // 128 + 1]))
                kvm[idx] = p.now("ve")

            for idx, (cc, tc) in enumerate(kidx):
                psk = kpsb[idx % 2]
                if idx >= 2:
                    p.wait("tensor", "ve", kvm[idx - 2])
                elif idx == 1:
                    p.wait("tensor", "ve", ve_bvb)
                else:
                    p.wait("tensor", "ve")
                extra = None
                if lora:
                    extra = (lambda e, cc=cc, tc=tc, psk=psk: e.matmul(
                        psk[:, :], BLG[0:8, GR + cc * 128: GR + cc * 128 + 128],
                        MIDB[0:8, tc * 512:(tc + 1) * 512], start=False, stop=True))
                ktm[idx] = mm_chain([
                    (lambda e, cc=cc, tc=tc, dc=dc, psk=psk: e.matmul(
                        psk[:, :], WQG[:, dc * 2 * GR + cc * 128: dc * 2 * GR + cc * 128 + 128],
                        XF[:, dc * Tl + tc * 512: dc * Tl + tc * 512 + 512],
                        start=(dc == 0), stop=(dc == ND - 1 and not lora)))
                    for dc in range(ND)], extra)
                if idx >= 1:
                    emit_kevac(idx - 1)
            emit_kevac(len(kidx) - 1)
            se_k = p.now("ve")
            if not lora:
                p.wait("vector", "ve", se_k)
                for cc in range(GC):
                    for tc in range(Tl // 512):
                        emit_rope((KG, slice(cc * Tl + tc * 512, cc * Tl + tc * 512 + 512)),
                                  g * GC + cc)
                p.wait("tensor", "ve")

            p.wait("vector", "ve", ve_bvb)
            vpsb = [ps_sa[0], ps_sb[0]]
            vtm = {}
            vvm = {}
            vt_view = VT[:, :].rearrange("q (nt h c) -> q nt h c", h=HG, c=65)

            def emit_vevac(kt):
                psv = vpsb[kt % 2]
                p.wait("vector", "te", vtm[kt])
                p.inc_after("vector", "ve", 1,
                            lambda e, kt=kt, psv=psv, vt_view=vt_view: e.tensor_tensor(
                                vt_view[:, kt, :, 0:64],
                                psv[:, 0:GR].rearrange("q (h c) -> q h c", c=64),
                                BVB[:, :].rearrange("q (h c) -> q h c", c=64), ALU.add))
                vvm[kt] = p.now("ve")

            for kt in range(NTl):
                psv = vpsb[kt % 2]
                if kt >= 2:
                    p.wait("tensor", "ve", vvm[kt - 2])
                else:
                    p.wait("tensor", "ve")
                extra = None
                if lora:
                    extra = (lambda e, kt=kt, psv=psv: e.matmul(
                        psv[:, 0:GR], MIDB[0:8, kt * 128:(kt + 1) * 128],
                        BLG[0:8, 2 * GR: 3 * GR], start=False, stop=True))
                vtm[kt] = mm_chain([
                    (lambda e, kt=kt, dc=dc, psv=psv: e.matmul(
                        psv[:, 0:GR], XF[:, dc * Tl + kt * 128: dc * Tl + kt * 128 + 128],
                        WQG[:, dc * 2 * GR + GR: dc * 2 * GR + 2 * GR],
                        start=(dc == 0), stop=(dc == ND - 1 and not lora)))
                    for dc in range(ND)], extra)
                if kt >= 1:
                    emit_vevac(kt - 1)
            emit_vevac(NTl - 1)
            ve_v = p.now("ve")
            p.wait("tensor", "ve", ve_v)

            p.wait("gpsimd", "te")
            p.inc_after("gpsimd", "gw", 16,
                        lambda e, g=g: e.dma_start(
                            out=WQG[:, :].rearrange("q (nd s c) -> q nd s c", s=2, c=GR)[:, :, 0, :],
                            in_=wqkv.rearrange("(dc q) f -> q dc f", q=128)[:, :, g * GR: g * GR + GR]))
            gw_q = p.now("gw")
            p.wait("tensor", "gw", gw_q)

            def emit_qprep(qs, g=g):
                qp = g * NQS + qs
                qfb = QFB[qp % 2]
                for cc in range(GC):
                    p.wait("tensor", "ve", state["qprep_psa_ve"])
                    extra = None
                    if lora:
                        extra = (lambda e, cc=cc, qs=qs: e.matmul(
                            ps_a[:, :], BLG[0:8, cc * 128: cc * 128 + 128],
                            MIDB[0:8, qs * 512:(qs + 1) * 512], start=False, stop=True))
                    te_c = mm_chain([
                        (lambda e, cc=cc, dc=dc, qs=qs: e.matmul(
                            ps_a[:, :], WQG[:, dc * 2 * GR + cc * 128: dc * 2 * GR + cc * 128 + 128],
                            XF[:, dc * Tl + qs * 512: dc * Tl + qs * 512 + 512],
                            start=(dc == 0), stop=(dc == ND - 1 and not lora)))
                        for dc in range(ND)], extra)
                    p.wait("vector", "te", te_c)
                    fglob = g * GR + cc * 128
                    p.inc_after("vector", "ve", 1,
                                lambda e, cc=cc, fglob=fglob, qfb=qfb: e.tensor_scalar_add(
                                    qfb[:, cc * 512:(cc + 1) * 512], ps_a[:, :],
                                    BQC[:, fglob // 128: fglob // 128 + 1]))
                    state["qprep_psa_ve"] = p.now("ve")
                if not lora:
                    for cc in range(GC):
                        sl = slice(cc * 512, cc * 512 + 512)
                        col = g * GC + cc
                        p.wait("tensor", "ve", state["rb_free_ve"])
                        te_r = mm_chain([
                            lambda e, qfb=qfb, sl=sl: e.matmul(
                                ps_rb[:, :], PERM[:, :], qfb[:, sl], start=True, stop=True)])
                        p.wait("vector", "ve")
                        p.inc_after("vector", "ve", 1,
                                    lambda e, qfb=qfb, sl=sl, col=col: e.tensor_scalar(
                                        RSC2[:, :], qfb[:, sl], COS[:, col:col + 1],
                                        None, ALU.mult))
                        p.wait("vector", "te", te_r)
                        p.wait("vector", "ve")
                        p.inc_after("vector", "ve", 1,
                                    lambda e, qfb=qfb, sl=sl, col=col: e.scalar_tensor_tensor(
                                        qfb[:, sl], ps_rb[:, :], SIN[:, col:col + 1],
                                        RSC2[:, :], ALU.mult, ALU.add))
                        state["rb_free_ve"] = p.now("ve")
                        state["qprep_psa_ve"] = max(state["qprep_psa_ve"], p.now("ve"))
                qprep_ready[qp] = p.now("ve")

            def push_tail(g, qs, cc, te_pv):
                p.wait("vector", "te", te_pv)
                p.wait("vector", "ve")
                p.inc_after("vector", "ve", 1,
                            lambda e: e.tensor_copy(OAS[0:65, :], ps_oa[0:65, :]))
                p.inc_after("vector", "ve", 1,
                            lambda e: e.tensor_copy(OBS[0:65, :], ps_ob[0:65, :]))
                state["psov_free_ve"] = p.now("ve")
                ev_mark = state["psov_free_ve"]

                def mk(hb, src32):
                    def c_recip():
                        p.wait("vector", "ve", ev_mark)
                        if hb == 1:
                            p.wait("vector", "te", state["rbmm" + str(0)])
                        p.inc_after("vector", "ve", 1,
                                    lambda e, src32=src32: e.reciprocal(
                                        RECIP, src32[64:65, :]))
                        state["recip" + str(hb)] = p.now("ve")

                    def c_rb():
                        p.wait("tensor", "ve", state["recip" + str(hb)])
                        p.wait("tensor", "ve", state["rb_free_ve"])
                        p.inc_after("tensor", "te", 1,
                                    lambda e, hb=hb: e.matmul(
                                        ps_rb[hb * 64:hb * 64 + 64, :], ONE1[0:1, 0:64],
                                        RECIP, start=True, stop=True))
                        state["rbmm" + str(hb)] = p.now("te")

                    def c_of():
                        p.wait("vector", "te", state["rbmm" + str(hb)])
                        grow = g * GR + (cc * 2 + hb) * 64
                        ocn, orb = grow // 128, grow % 128
                        p.inc_after("vector", "ve", 1,
                                    lambda e, ocn=ocn, orb=orb, qs=qs, hb=hb, src32=src32:
                                    e.tensor_tensor(
                                        OF[orb:orb + 64,
                                           ocn * Tl + qs * 512: ocn * Tl + qs * 512 + 512],
                                        src32[0:64, :], ps_rb[hb * 64:hb * 64 + 64, :],
                                        ALU.mult))
                        state["rb_free_ve"] = p.now("ve")
                    return [c_recip, c_rb, c_of]

                pending.extend(mk(0, OAS))
                pending.extend(mk(1, OBS))

            emit_qprep(0)
            for qs in range(NQS):
                qp = g * NQS + qs
                qfb = QFB[qp % 2]
                p.wait("tensor", "ve", qprep_ready[qp])
                for cc in range(GC):
                    kts = list(range(NTl)) if lora else list(range(min(NTl, qs * 4 + 4)))
                    se_exp = {}
                    ve_msk = {}

                    def emit_st_exp(i, kt):
                        if i >= 2:
                            p.wait("tensor", "se", se_exp[i - 2])
                            if (not lora) and kts[i - 2] >= qs * 4:
                                p.wait("tensor", "ve", ve_msk[i - 2])
                        p.inc_after("tensor", "te", 1,
                                    lambda e, kt=kt, cc=cc, i=i, qfb=qfb: e.matmul(
                                        ps_sa[i % 2][:, :],
                                        KG[0:64, cc * Tl + kt * 128: cc * Tl + kt * 128 + 128],
                                        qfb[0:64, cc * 512:(cc + 1) * 512],
                                        start=True, stop=True))
                        p.inc_after("tensor", "te", 1,
                                    lambda e, kt=kt, cc=cc, i=i, qfb=qfb: e.matmul(
                                        ps_sb[i % 2][:, :],
                                        KG[64:128, cc * Tl + kt * 128: cc * Tl + kt * 128 + 128],
                                        qfb[64:128, cc * 512:(cc + 1) * 512],
                                        start=True, stop=True))
                        te_s = p.now("te")
                        p.wait("scalar", "te", te_s)
                        p.s(lambda e, i=i: e.activation(
                            PB[i % 2][:, :], ps_sa[i % 2][:, :], AF.Exp, scale=0.125))
                        p.inc_after("scalar", "se", 1,
                                    lambda e, i=i: e.activation(
                                        PB2[i % 2][:, :], ps_sb[i % 2][:, :], AF.Exp, scale=0.125))
                        se_exp[i] = p.now("se")
                        if (not lora) and kt >= qs * 4:
                            mt = kt - qs * 4
                            p.wait("vector", "se", se_exp[i])
                            for PBx in (PB, PB2):
                                if mt > 0:
                                    p.inc_after("vector", "ve", 1,
                                                lambda e, i=i, mt=mt, PBx=PBx: e.memset(
                                                    PBx[i % 2][:, 0:128 * mt], 0.0))
                                p.inc_after("vector", "ve", 1,
                                            lambda e, i=i, mt=mt, PBx=PBx: e.tensor_tensor(
                                                PBx[i % 2][:, 128 * mt:128 * mt + 128],
                                                PBx[i % 2][:, 128 * mt:128 * mt + 128],
                                                MASK[:, :], ALU.mult))
                            ve_msk[i] = p.now("ve")

                    def emit_pv(i, kt, last):
                        if i == 0:
                            p.wait("tensor", "ve", state["psov_free_ve"])
                        if (not lora) and kt >= qs * 4:
                            p.wait("tensor", "ve", ve_msk[i])
                        else:
                            p.wait("tensor", "se", se_exp[i])
                        ha = (cc * 2) % HG
                        fn_a = (lambda e, kt=kt, i=i, ha=ha, last=last: e.matmul(
                            ps_oa[0:65, :],
                            VT[:, kt * HG * 65 + ha * 65: kt * HG * 65 + ha * 65 + 65],
                            PB[i % 2][:, :], start=(i == 0), stop=last))
                        fn_b = (lambda e, kt=kt, i=i, ha=ha, last=last: e.matmul(
                            ps_ob[0:65, :],
                            VT[:, kt * HG * 65 + (ha + 1) * 65: kt * HG * 65 + (ha + 1) * 65 + 65],
                            PB2[i % 2][:, :], start=(i == 0), stop=last))
                        p.t(fn_a)
                        if last:
                            p.inc_after("tensor", "te", 1, fn_b)
                        else:
                            p.t(fn_b)

                    nk = len(kts)
                    for i, kt in enumerate(kts):
                        if i > 1:
                            emit_pv(i - 2, kts[i - 2], False)
                        emit_st_exp(i, kt)
                        if i >= 2 and i % 2 == 0 and pending:
                            pending.pop(0)()
                    if nk >= 2:
                        emit_pv(nk - 2, kts[nk - 2], False)
                    emit_pv(nk - 1, kts[nk - 1], True)
                    te_pv = p.now("te")
                    while pending:
                        pending.pop(0)()
                    push_tail(g, qs, cc, te_pv)

                if qs + 1 < NQS:
                    emit_qprep(qs + 1)
            while pending:
                pending.pop(0)()
        ve_of = p.now("ve")
        p.wait("tensor", "ve", ve_of)

        if lora:
            for tc in range(Tl // 512):
                p.wait("tensor", "se")
                mm_chain([
                    (lambda e, tc=tc, dc=dc: e.matmul(
                        ps_a[0:8, :], APJ[:, dc * 8:(dc + 1) * 8],
                        OF[:, dc * Tl + tc * 512: dc * Tl + tc * 512 + 512],
                        start=(dc == 0), stop=(dc == ND - 1)))
                    for dc in range(ND)])
                p.wait("scalar", "te")
                p.inc_after("scalar", "se", 1,
                            lambda e, tc=tc: e.activation(
                                MIDB[0:8, tc * 512:(tc + 1) * 512], ps_a[0:8, :],
                                AF.Copy, scale=2.0))
        se_mid2 = p.now("se")

        p.wait("sync", "ve")
        p.inc_after("sync", "ld", 16, lambda e: e.dma_start(out=BPR[0:1, :], in_=bproj))
        p.wait("gpsimd", "te")
        p.inc_after("gpsimd", "gwp", 16,
                    lambda e: e.dma_start(out=WPROJ[:, 0:ND * D].rearrange("q (dc f) -> q dc f", f=D),
                                          in_=wproj.rearrange("(dc q) f -> q dc f", q=128)))
        gld_wp = p.now("gwp")
        p.wait("tensor", "gwp", gld_wp)
        p.wait("tensor", "ld")
        for half in range(2):
            p.wait("tensor", "ve")
            mm_chain([lambda e, half=half: e.matmul(
                ps_rb[:, :], ONE1[0:1, :], BPR[0:1, half * 512:(half + 1) * 512],
                start=True, stop=True)])
            p.wait("vector", "te")
            p.inc_after("vector", "ve", 1,
                        lambda e, half=half: e.tensor_copy(BPB[:, half * 512:(half + 1) * 512],
                                                           ps_rb[:, :]))
        ve_bpb = p.now("ve")

        ps_at_ctx.close()
        ps_po = ctx.enter_context(nc.psum_tensor(p.pfx + "ps_po", [128, D], F32))
        p.wait("tensor", "se", se_mid2)
        p.wait("tensor", "ve", ve_bpb)
        for nt in range(NTl):
            buf = (nt % 2) * D
            p.wait("sync", "ve")
            p.inc_after("sync", "ld", 16,
                        lambda e, nt=nt, buf=buf: e.dma_start(out=S32[:, buf:buf + D], in_=x_t[nt]))
            ld_nt = p.now("ld")
            if nt >= 1:
                p.wait("tensor", "ve")
            for half in range(2):
                extra = None
                if lora:
                    extra = (lambda e, nt=nt, half=half: e.matmul(
                        ps_po[:, half * 512:(half + 1) * 512],
                        MIDB[0:8, nt * 128:(nt + 1) * 128],
                        BLP[0:8, half * 512: half * 512 + 512], start=False, stop=True))
                mm_chain([
                    (lambda e, nt=nt, half=half, oc=oc: e.matmul(
                        ps_po[:, half * 512:(half + 1) * 512],
                        OF[:, oc * Tl + nt * 128: oc * Tl + nt * 128 + 128],
                        WPROJ[:, oc * D + half * 512: oc * D + half * 512 + 512],
                        start=(oc == 0), stop=(oc == ND - 1 and not lora)))
                    for oc in range(ND)], extra)
            te_po = p.now("te")
            p.wait("vector", "te", te_po)
            p.wait("vector", "ld", ld_nt)
            p.inc_after("vector", "ve", 1,
                        lambda e, buf=buf: e.tensor_tensor(S32[:, buf:buf + D], S32[:, buf:buf + D],
                                                           ps_po[:, :], ALU.add))
            p.wait("vector", "ve")
            p.inc_after("vector", "ve", 1,
                        lambda e, buf=buf: e.tensor_tensor(S32[:, buf:buf + D], S32[:, buf:buf + D],
                                                           BPB[:, :], ALU.add))
            p.wait("scalar", "st")
            p.wait("vector", "st")
            emit_ln_tile(p, S32[:, buf:buf + D], OUT[:, 0:D], stats,
                         sq_scr=OUT[:, 0:D])
            ve_out = p.now("ve")
            p.wait("sync", "ve", ve_out)
            p.inc_after("sync", "st", 16,
                        lambda e, nt=nt: e.dma_start(out=yo_t[nt], in_=OUT[:, 0:D]))

        p.wait("sync", "st")
        p.barrier()
    if standalone:
        p.run()
        _octx.__exit__(None, None, None)
    return nc



def build_mamba(T_=None, p=None, dram=None):
    Tl = T_ or T
    DI = 2048
    NDI = DI // 128
    NST = 16
    TS = 256
    NS = Tl // TS
    TPS = TS // 128

    standalone = p is None
    if standalone:
        _nc = bass.Bass("TRN2", target_bir_lowering=False)
        _octx = ExitStack()
        _octx.__enter__()
        p = Prog(_nc, _octx)
        p.pfx = ""
        dram = lambda name, shape, dt, kind: _nc.dram_tensor(name, shape, dt, kind=kind).ap()
    nc = p.nc
    x = dram("x", [Tl, D], F32, "ExternalInput")
    lnw = dram("lnw", [1, D], F32, "ExternalInput")
    lnb = dram("lnb", [1, D], F32, "ExternalInput")
    wipT = dram("wipT", [D, 2 * DI], F32, "ExternalInput")
    convw = dram("convw", [128, NDI * 4], F32, "ExternalInput")
    convb = dram("convb", [128, NDI], F32, "ExternalInput")
    wxpT = dram("wxpT", [DI, 96], F32, "ExternalInput")
    wdtT = dram("wdtT", [64, DI], F32, "ExternalInput")
    dtb = dram("dtb", [128, NDI], F32, "ExternalInput")
    alog = dram("alog", [128, NDI * NST], F32, "ExternalInput")
    dmat = dram("dmat", [128, NDI], F32, "ExternalInput")
    wopT = dram("wopT", [DI, D], F32, "ExternalInput")
    ehb = dram("c_ehb", [32, NST * 128], F32, "ExternalInput")
    ehc = dram("c_ehc", [32, NST * 128], F32, "ExternalInput")
    ident = dram("c_ident", [128, 128], F32, "ExternalInput")
    yo = dram("y", [Tl, D], F32, "ExternalOutput")

    prev_scan_marks = None
    prev_yf_marks = None
    prev_carry_mark = None
    x_t = x.rearrange("(nt p) d -> nt p d", p=128)
    yo_t = yo.rearrange("(nt p) d -> nt p d", p=128)

    with ExitStack() as ctx:
        sbt = lambda n, *a: ctx.enter_context(nc.sbuf_tensor(p.pfx + n, *a))
        pst = lambda n, *a: ctx.enter_context(nc.psum_tensor(p.pfx + n, *a))

        IDN = sbt("IDN", [128, 128], BF16)
        WIP = sbt("WIP", [128, ND * 2 * DI], BF16)
        WOP = sbt("WOP", [128, NDI * D], BF16)
        WXP = sbt("WXP", [128, NDI * 96], BF16)
        WDT = sbt("WDT", [64, NDI * 128], BF16)
        CW = sbt("CW", [128, NDI * 4], F32)
        CB = sbt("CB", [128, NDI], F32)
        DTB = sbt("DTB", [128, NDI], F32)
        AC = sbt("AC", [128, NDI * NST], F32)
        DM = sbt("DM", [128, NDI], F32)
        EHB = sbt("EHB", [32, NST * 128], BF16)
        EHC = sbt("EHC", [32, NST * 128], BF16)
        XS32 = sbt("XS32m", [128, TPS * D], F32)
        XLNF = sbt("XLNF", [128, ND * TS], BF16)
        XCB = sbt("XCB", [128, NDI * (TS + 3)], BF16)
        ZS = sbt("ZSm", [128, NDI * TS], BF16)
        XCONV = sbt("XCONV", [128, NDI * TS], BF16)
        DELTA = sbt("DELTA", [128, NDI * TS], BF16)
        DU = sbt("DU", [128, NDI * TS], BF16)
        XLNT = DU[:, 0:TPS * D]
        BB = sbt("BBm", [128, NST * TS], BF16)
        CCB = sbt("CCm", [128, NST * TS], BF16)
        DT_ = sbt("DTm", [64, TS], BF16)
        BCB = sbt("BCB", [32, TS], BF16)
        DA = [sbt(f"DAm{i}", [128, TS], F32) for i in range(2)]
        HSB = [sbt(f"HSBm{i}", [128, TS], F32) for i in range(3)]
        YAG = sbt("YAGm", [128, TS], F32)
        TS3G = sbt("TS3Gm", [128, TS], F32)
        DBu = [sbt(f"DBm{i}", [128, TS], BF16) for i in range(2)]
        YA = sbt("YAm", [128, TS], F32)
        SCR1 = sbt("SCR1m", [128, TS], F32)
        TS3 = SCR1
        CV = SCR1
        AB = SCR1
        YF = sbt("YFm", [128, NDI * TS], BF16)
        HC = sbt("HCm", [128, NDI * NST], F32)
        EL = sbt("ELm", [128, TS], F32)
        D0 = sbt("D0m", [128, TS], F32)
        WLNB = sbt("WLNB", [128, D], BF16)
        BLNB = sbt("BLNB", [128, D], BF16)
        ONE1 = sbt("ONE1m", [1, 128], F32)
        OUT = XLNF[:, :].bitcast(F32)
        LNROW = OUT[:, 0:512].bitcast(F32)
        stats = alloc_stats(sbt)

        ps_tr = pst("ps_tr", [128, ND * 128], BF16)
        ps_ipb = [pst(f"ps_ip{i}", [128, TS], F32) for i in range(2)]
        ps_xp = pst("ps_xp", [128, TS], F32)
        ps_dt = pst("ps_dt", [128, TS], F32)
        ps_bc = pst("ps_bc", [128, TS], F32)
        ps_o = pst("ps_o", [128, D], F32)

        p.inc_after("vector", "ve", 1, lambda e: e.memset(stats["eps"][:, 0:1], LN_EPS))
        p.inc_after("vector", "ve", 1, lambda e: e.memset(ONE1[:, :], 1.0))
        p.inc_after("vector", "ve", 1, lambda e: e.memset(HC[:, :], 0.0))
        p.inc_after("vector", "ve", 1, lambda e: e.memset(
            XCB[:, :].rearrange("q (dc c) -> q dc c", c=TS + 3)[:, :, 0:3], 0.0))

        p.inc_after("gpsimd", "gld", 16, lambda e: e.dma_start(out=IDN[:, :], in_=ident))
        p.inc_after("gpsimd", "gld", 16, lambda e: e.dma_start(out=EHB[:, :], in_=ehb))
        p.inc_after("gpsimd", "gld", 16, lambda e: e.dma_start(out=EHC[:, :], in_=ehc))
        for piece in range(4):
            p.inc_after("gpsimd", "gld", 16,
                        lambda e, piece=piece: e.dma_start(
                            out=WIP[:, :].rearrange("q (dc f) -> q dc f", f=2 * DI)[:, 2 * piece:2 * piece + 2, :],
                            in_=wipT.rearrange("(dc q) f -> q dc f", q=128)[:, 2 * piece:2 * piece + 2, :]))
        for piece in range(4):
            p.inc_after("gpsimd", "gld", 16,
                        lambda e, piece=piece: e.dma_start(
                            out=WOP[:, :].rearrange("q (fc f) -> q fc f", f=D)[:, 4 * piece:4 * piece + 4, :],
                            in_=wopT.rearrange("(fc q) f -> q fc f", q=128)[:, 4 * piece:4 * piece + 4, :]))
        p.inc_after("gpsimd", "gld", 16,
                    lambda e: e.dma_start(out=WXP[:, :].rearrange("q (dc f) -> q dc f", f=96),
                                          in_=wxpT.rearrange("(dc q) f -> q dc f", q=128)))
        p.inc_after("gpsimd", "gld", 16, lambda e: e.dma_start(out=WDT[0:64, :], in_=wdtT))
        p.inc_after("sync", "cld", 16, lambda e: e.dma_start(out=CW[:, :], in_=convw))
        p.inc_after("sync", "cld", 16, lambda e: e.dma_start(out=CB[:, :], in_=convb))
        p.inc_after("sync", "cld", 16, lambda e: e.dma_start(out=DTB[:, :], in_=dtb))
        p.inc_after("sync", "cld", 16, lambda e: e.dma_start(out=AC[:, :], in_=alog))
        p.inc_after("sync", "cld", 16, lambda e: e.dma_start(out=DM[:, :], in_=dmat))
        ld_c = p.now("cld")
        p.wait("scalar", "cld", ld_c)
        p.inc_after("scalar", "se", 1, lambda e: e.activation(AC[:, :], AC[:, :], AF.Exp))
        p.wait("vector", "se")
        p.inc_after("vector", "ve", 1, lambda e: e.tensor_scalar_mul(AC[:, :], AC[:, :], -1.0))
        ve_a = p.now("ve")
        LNW = XS32[:, :].bitcast(F32)[0:1, 0:D]
        LNB2 = XS32[:, :].bitcast(F32)[0:1, D:2 * D]
        p.inc_after("sync", "lnld", 16, lambda e: e.dma_start(out=LNW, in_=lnw))
        p.inc_after("sync", "lnld", 16, lambda e: e.dma_start(out=LNB2, in_=lnb))
        ld_ln = p.now("lnld")
        p.wait("tensor", "lnld", ld_ln)
        p.wait("tensor", "ve", ve_a)
        for half in range(2):
            p.inc_after("tensor", "te", 1,
                        lambda e, half=half: e.matmul(ps_o[:, 0:512], ONE1[0:1, :],
                                                      LNW[0:1, half * 512:(half + 1) * 512],
                                                      start=True, stop=True))
            p.wait("vector", "te")
            p.inc_after("vector", "ve", 1,
                        lambda e, half=half: e.tensor_copy(WLNB[:, half * 512:(half + 1) * 512],
                                                           ps_o[:, 0:512]))
            p.wait("tensor", "ve")
            p.inc_after("tensor", "te", 1,
                        lambda e, half=half: e.matmul(ps_o[:, 0:512], ONE1[0:1, :],
                                                      LNB2[0:1, half * 512:(half + 1) * 512],
                                                      start=True, stop=True))
            p.wait("vector", "te")
            p.inc_after("vector", "ve", 1,
                        lambda e, half=half: e.tensor_copy(BLNB[:, half * 512:(half + 1) * 512],
                                                           ps_o[:, 0:512]))
            p.wait("tensor", "ve")
        ve_pre = p.now("ve")
        p.wait("sync", "ve", ve_pre)

        for s in range(NS):
            p.wait("sync", "ve")
            for tt in range(TPS):
                p.inc_after("sync", "ld", 16,
                            lambda e, s=s, tt=tt: e.dma_start(out=XS32[:, tt * D:(tt + 1) * D],
                                                              in_=x_t[s * TPS + tt]))
            ld_s = p.now("ld")
            for tt in range(TPS):
                p.wait("vector", "ld", ld_s)
                p.wait("scalar", "ld", ld_s)
                p.wait("scalar", "st")
                p.wait("vector", "st")
                emit_ln_tile(p, XS32[:, tt * D:(tt + 1) * D], OUT[:, 0:D], stats,
                             sq_scr=OUT[:, 0:D])
                p.wait("vector", "ve")
                p.inc_after("vector", "ve", 1,
                            lambda e, tt=tt: e.tensor_tensor(OUT[:, 0:D], OUT[:, 0:D],
                                                             WLNB[:, :], ALU.mult))
                p.wait("vector", "ve")
                p.inc_after("vector", "ve", 1,
                            lambda e, tt=tt: e.tensor_tensor(XLNT[:, tt * D:(tt + 1) * D],
                                                             OUT[:, 0:D], BLNB[:, :], ALU.add))
            ve_ln = p.now("ve")
            p.wait("tensor", "gld")
            p.wait("tensor", "ve", ve_ln)
            xlnf_view = XLNF[:, :].rearrange("q (nd t) -> q nd t", t=TS)
            p.wait("vector", "st")
            for tt in range(TPS):
                p.wait("tensor", "ve")
                fns = [
                    (lambda e, tt=tt, dc=dc: e.transpose(
                        ps_tr[:, dc * 128:(dc + 1) * 128],
                        XLNT[:, tt * D + dc * 128: tt * D + (dc + 1) * 128], IDN[:, :]))
                    for dc in range(ND)]
                for i, fn in enumerate(fns):
                    if i == ND - 1:
                        p.inc_after("tensor", "te", 1, fn)
                    else:
                        p.t(fn)
                p.wait("vector", "te")
                p.inc_after("vector", "ve", 1,
                            lambda e, tt=tt, xlnf_view=xlnf_view: e.tensor_copy(
                                xlnf_view[:, :, tt * 128:(tt + 1) * 128],
                                ps_tr[:, :].rearrange("q (nd f) -> q nd f", f=128)))
            ve_xf = p.now("ve")
            p.wait("tensor", "ve", ve_xf)
            p.wait("tensor", "gld")
            xcb_view = XCB[:, :].rearrange("q (dc c) -> q dc c", c=TS + 3)
            se_ip_marks = {}
            ve_ip_marks = {}
            se_silu_marks = {}
            CVb = [SCR1, YA]
            for fc in range(NDI):
                ps_ip = ps_ipb[fc % 2]
                if fc >= 2:
                    p.wait("tensor", "ve", ve_ip_marks[fc - 2])
                elif s > 0:
                    p.wait("tensor", "se")
                    p.wait("tensor", "ve")
                for dc in range(ND):
                    fn = lambda e, fc=fc, dc=dc, ps_ip=ps_ip: e.matmul(
                        ps_ip[:, :], WIP[:, (dc * 32 + fc) * 128:(dc * 32 + fc + 1) * 128],
                        XLNF[:, dc * TS:(dc + 1) * TS],
                        start=(dc == 0), stop=(dc == ND - 1))
                    if dc == ND - 1:
                        p.inc_after("tensor", "te", 1, fn)
                    else:
                        p.t(fn)
                if True:
                    p.wait("vector", "te")
                    p.inc_after("vector", "ve", 1,
                                lambda e, fc=fc, xcb_view=xcb_view, ps_ip=ps_ip: e.tensor_copy(
                                    xcb_view[:, fc, 3:TS + 3], ps_ip[:, :]))
                    ve_ip_marks[fc] = p.now("ve")
                    cv = CVb[fc % 2]
                    if fc >= 2:
                        p.wait("vector", "se", se_silu_marks[fc - 2])
                    p.wait("vector", "ve")
                    p.inc_after("vector", "ve", 1,
                                lambda e, fc=fc, cv=cv: e.tensor_scalar(
                                    cv[:, :], XCB[:, fc * (TS + 3): fc * (TS + 3) + TS],
                                    CW[:, fc * 4: fc * 4 + 1], None, ALU.mult))
                    for k in range(1, 4):
                        p.wait("vector", "ve")
                        p.inc_after("vector", "ve", 1,
                                    lambda e, fc=fc, k=k, cv=cv: e.scalar_tensor_tensor(
                                        cv[:, :], XCB[:, fc * (TS + 3) + k: fc * (TS + 3) + k + TS],
                                        CW[:, fc * 4 + k: fc * 4 + k + 1], cv[:, :],
                                        ALU.mult, ALU.add))
                    ve_cv = p.now("ve")
                    p.wait("scalar", "ve", ve_cv)
                    p.inc_after("scalar", "se", 1,
                                lambda e, fc=fc, cv=cv: e.activation(
                                    XCONV[:, fc * TS:(fc + 1) * TS], cv[:, :], AF.Silu,
                                    bias=CB[:, fc:fc + 1]))
                    se_silu_marks[fc] = p.now("se")
                se_ip_marks[fc] = p.now("se")
            se_cv = p.now("se")
            p.wait("vector", "se", se_cv)
            p.inc_after("vector", "ve", 1,
                        lambda e: e.tensor_copy(xcb_view[:, :, 0:3], xcb_view[:, :, TS:TS + 3]))
            p.wait("tensor", "se", se_cv)
            for dc in range(NDI):
                fn = lambda e, dc=dc: e.matmul(
                    ps_xp[0:96, :], WXP[:, dc * 96:(dc + 1) * 96],
                    XCONV[:, dc * TS:(dc + 1) * TS],
                    start=(dc == 0), stop=(dc == NDI - 1))
                if dc == NDI - 1:
                    p.inc_after("tensor", "te", 1, fn)
                else:
                    p.t(fn)
            te_xp = p.now("te")
            p.wait("scalar", "te", te_xp)
            p.inc_after("scalar", "se", 1,
                        lambda e: e.activation(DT_[0:64, :], ps_xp[0:64, :], AF.Copy))
            p.wait("scalar", "se")
            p.inc_after("scalar", "se", 1,
                        lambda e: e.activation(BCB[0:32, :], ps_xp[64:96, :], AF.Copy))
            se_bc = p.now("se")

            sp_mark = {}
            z_marks = {}
            bb_mark = {}
            cc_mark = {}
            d0_marks = {}
            bc_last = {"se": 0}

            def emit_softplus_A(fc):
                if fc == 0:
                    p.wait("tensor", "se", se_bc)
                if fc - 1 in d0_marks:
                    p.wait("tensor", "ve", d0_marks[fc - 1])
                p.inc_after("tensor", "te", 1,
                            lambda e, fc=fc: e.matmul(ps_dt[:, :],
                                                      WDT[0:64, fc * 128:(fc + 1) * 128],
                                                      DT_[0:64, :], start=True, stop=True))
                p.wait("vector", "te")
                p.wait("vector", "ve")
                p.inc_after("vector", "ve", 1,
                            lambda e, fc=fc: e.tensor_scalar_add(D0[:, :], ps_dt[:, :],
                                                                 DTB[:, fc:fc + 1]))
                d0_marks[fc] = p.now("ve")
                p.wait("vector", "ve")
                p.inc_after("vector", "ve", 1,
                            lambda e, fc=fc: e.tensor_scalar_max(
                                DELTA[:, fc * TS:(fc + 1) * TS], D0[:, :], 0.0))
                p.wait("vector", "ve")
                p.inc_after("vector", "ve", 1,
                            lambda e: e.scalar_tensor_tensor(D0[:, :], D0[:, :], -1.0, D0[:, :],
                                                             ALU.mult, ALU.max))
                vm = p.now("ve")
                p.wait("scalar", "ve", vm)
                p.inc_after("scalar", "se", 1,
                            lambda e: e.activation(D0[:, :], D0[:, :], AF.Exp, scale=-1.0))

            def emit_softplus_A2(fc):
                p.wait("scalar", "se")
                p.inc_after("scalar", "se", 1,
                            lambda e: e.activation(D0[:, :], D0[:, :], AF.Ln, bias=1.0))
                sp_mark[fc] = p.now("se")

            def emit_softplus_B(fc):
                p.wait("vector", "se", sp_mark[fc])
                p.inc_after("vector", "ve", 1,
                            lambda e, fc=fc: e.tensor_tensor(
                                DELTA[:, fc * TS:(fc + 1) * TS],
                                DELTA[:, fc * TS:(fc + 1) * TS], D0[:, :], ALU.add))

            def emit_du(dcx):
                p.wait("vector", "ve")
                p.inc_after("vector", "ve", 1,
                            lambda e, dcx=dcx: e.tensor_tensor(
                                DU[:, dcx * TS:(dcx + 1) * TS],
                                DELTA[:, dcx * TS:(dcx + 1) * TS],
                                XCONV[:, dcx * TS:(dcx + 1) * TS], ALU.mult))

            def emit_z(fc):
                ps_ip = ps_ipb[fc % 2]
                if fc >= 18:
                    p.wait("tensor", "se", z_marks[fc - 2])
                else:
                    p.wait("tensor", "ve", ve_ip_marks[fc - 2])
                for dc in range(ND):
                    fn = lambda e, fc=fc, dc=dc, ps_ip=ps_ip: e.matmul(
                        ps_ip[:, :], WIP[:, (dc * 32 + fc) * 128:(dc * 32 + fc + 1) * 128],
                        XLNF[:, dc * TS:(dc + 1) * TS],
                        start=(dc == 0), stop=(dc == ND - 1))
                    if dc == ND - 1:
                        p.inc_after("tensor", "te", 1, fn)
                    else:
                        p.t(fn)
                p.wait("scalar", "te")
                if prev_yf_marks is not None:
                    p.wait("scalar", "ve", prev_yf_marks[fc - NDI])
                p.inc_after("scalar", "se", 1,
                            lambda e, fc=fc, ps_ip=ps_ip: e.activation(
                                ZS[:, (fc - NDI) * TS:(fc - NDI + 1) * TS], ps_ip[:, :],
                                AF.Silu))
                z_marks[fc] = p.now("se")

            def emit_bbcc(n):
                if bc_last["se"]:
                    p.wait("tensor", "se", bc_last["se"])
                p.inc_after("tensor", "te", 1,
                            lambda e, n=n: e.matmul(ps_bc[:, :], EHB[:, n * 128:(n + 1) * 128],
                                                    BCB[0:32, :], start=True, stop=True))
                p.wait("scalar", "te")
                p.inc_after("scalar", "se", 1,
                            lambda e, n=n: e.activation(BB[:, n * TS:(n + 1) * TS],
                                                        ps_bc[:, :], AF.Copy))
                bb_mark[n] = p.now("se")
                p.wait("tensor", "se", bb_mark[n])
                p.inc_after("tensor", "te", 1,
                            lambda e, n=n: e.matmul(ps_bc[:, :], EHC[:, n * 128:(n + 1) * 128],
                                                    BCB[0:32, :], start=True, stop=True))
                p.wait("scalar", "te")
                p.inc_after("scalar", "se", 1,
                            lambda e, n=n: e.activation(CCB[:, n * TS:(n + 1) * TS],
                                                        ps_bc[:, :], AF.Copy))
                cc_mark[n] = p.now("se")
                bc_last["se"] = cc_mark[n]

            for fc in range(3):
                emit_softplus_A(fc)
                emit_softplus_A2(fc)
                emit_softplus_B(fc)
            emit_du(0)
            emit_du(1)
            emit_z(16)
            for n in range(8):
                emit_bbcc(n)

            extra = {}
            for k, sl in enumerate((0, 1, 2, 4, 6, 8, 10, 12)):
                extra.setdefault((0, sl), []).append(lambda n=8 + k: emit_bbcc(n))
            for dcx in range(1, 14):
                extra.setdefault((dcx, 2), []).append(
                    lambda dcx=dcx: emit_softplus_A(dcx + 2))
                extra.setdefault((dcx, 4), []).append(
                    lambda dcx=dcx: emit_softplus_A2(dcx + 2))
                extra.setdefault((dcx, 8), []).append(
                    lambda dcx=dcx: emit_softplus_B(dcx + 2))
            for dcx in range(15):
                extra.setdefault((dcx, 6), []).append(lambda dcx=dcx: emit_z(17 + dcx))
                extra.setdefault((dcx, 12), []).append(lambda dcx=dcx: emit_du(dcx + 1))

            scan_marks = {}
            yf_marks = {}
            carry_marks = {}
            G = NDI * NST
            GPN = (1, 3, 5, 7, 9, 11, 13, 15)
            gp_req = {}

            da_marks = {}

            def emit_da(gj):
                dcj, nj = gj // NST, gj % NST
                parj = gj % 2
                if gj >= 2:
                    p.wait("scalar", "ve", scan_marks[gj - 2])
                elif prev_scan_marks is not None:
                    p.wait("scalar", "ve", prev_scan_marks[G - 2 + gj])
                p.inc_after("scalar", "se", 1,
                            lambda e, dcj=dcj, nj=nj, parj=parj: e.activation(
                                DA[parj][:, :], DELTA[:, dcj * TS:(dcj + 1) * TS], AF.Exp,
                                scale=AC[:, dcj * NST + nj: dcj * NST + nj + 1]))
                da_marks[gj] = p.now("se")

            def emit_carry(gi):
                hsp = HSB[gi % 3]
                p.wait("scalar", "ve", scan_marks[gi])
                p.inc_after("scalar", "se", 1,
                            lambda e, gi=gi, hsp=hsp: e.activation(
                                HC[:, gi:gi + 1], hsp[:, TS - 1:TS], AF.Copy))
                carry_marks[gi] = p.now("se")

            def emit_dbu(gi):
                dcb, nb = gi // NST, gi % NST
                if dcb == 0:
                    p.wait("vector", "se", bb_mark[nb])
                p.inc_after("vector", "ve", 1,
                            lambda e, dcb=dcb, nb=nb, par=gi % 2: e.tensor_tensor(
                                DBu[par][:, :], DU[:, dcb * TS:(dcb + 1) * TS],
                                BB[:, nb * TS:(nb + 1) * TS], ALU.mult))

            TS3B = [TS3, EL]
            emit_dbu(0)
            emit_dbu(1)

            def emit_gp_slice(n, gi, hsb, dc):
                first = n == GPN[0]
                if dc == 0:
                    p.wait("gpsimd", "se", cc_mark[n])
                p.wait("gpsimd", "ve", scan_marks[gi])
                if first:
                    p.inc_after("gpsimd", "gpm", 1,
                                lambda e, n=n, hsb=hsb: e.tensor_tensor(
                                    YAG[:, :], hsb[:, :],
                                    CCB[:, n * TS:(n + 1) * TS], ALU.mult))
                else:
                    p.wait("gpsimd", "gpm")
                    p.inc_after("gpsimd", "gpm", 1,
                                lambda e, n=n, hsb=hsb: e.tensor_tensor(
                                    TS3G[:, :], hsb[:, :],
                                    CCB[:, n * TS:(n + 1) * TS], ALU.mult))
                    p.wait("gpsimd", "gpm")
                    p.inc_after("gpsimd", "gpm", 1,
                                lambda e: e.tensor_tensor(
                                    YAG[:, :], YAG[:, :], TS3G[:, :], ALU.add))
                gp_req[gi] = p.now("gpm")

            for dc in range(NDI):
                vy = 0
                pend_add = None
                for npair in range(NST // 2):
                    n0, n1 = 2 * npair, 2 * npair + 1
                    gi0, gi1 = dc * NST + n0, dc * NST + n1
                    hsb0, hsb1 = HSB[gi0 % 3], HSB[gi1 % 3]
                    if gi0 == 0:
                        emit_da(0)
                        emit_da(1)
                    if gi0 >= 2:
                        emit_carry(gi0 - 2)
                        emit_carry(gi1 - 2)
                    se_need = da_marks[gi1]
                    for gx in (gi0 - 3, gi1 - 3):
                        if gx >= 0 and gx in carry_marks:
                            se_need = max(se_need, carry_marks[gx])
                    if gi0 < 3 and prev_carry_mark is not None:
                        se_need = max(se_need, prev_carry_mark)
                    p.wait("vector", "se", se_need)
                    p.wait("vector", "ve")
                    gp_need = 0
                    for gx in (gi0 - 3, gi1 - 3):
                        if gx in gp_req:
                            gp_need = max(gp_need, gp_req[gx])
                    if gp_need:
                        p.wait("vector", "gpm", gp_need)
                    p.inc_after("vector", "ve", 1,
                                lambda e, gi=gi0, hsb=hsb0: e.tensor_tensor_scan(
                                    hsb[:, :], DA[gi % 2][:, :], DBu[gi % 2][:, :],
                                    HC[:, gi:gi + 1], ALU.mult, ALU.add))
                    scan_marks[gi0] = p.now("ve")
                    p.inc_after("vector", "ve", 1,
                                lambda e, gi=gi1, hsb=hsb1: e.tensor_tensor_scan(
                                    hsb[:, :], DA[gi % 2][:, :], DBu[gi % 2][:, :],
                                    HC[:, gi:gi + 1], ALU.mult, ALU.add))
                    scan_marks[gi1] = p.now("ve")
                    if gi0 + 2 < G:
                        emit_da(gi0 + 2)
                    if gi1 + 2 < G:
                        emit_da(gi1 + 2)
                    for n in (n0, n1):
                        for fn_x in extra.get((dc, n), ()):
                            fn_x()
                    if dc == 0:
                        p.wait("vector", "se", cc_mark[n0])
                    p.wait("vector", "ve", scan_marks[gi0])
                    if n0 == 0:
                        p.inc_after("vector", "ve", 1,
                                    lambda e, n=n0, hsb=hsb0: e.tensor_tensor(
                                        YA[:, :], hsb[:, :],
                                        CCB[:, n * TS:(n + 1) * TS], ALU.mult))
                    else:
                        t3 = TS3B[vy % 2]
                        p.inc_after("vector", "ve", 1,
                                    lambda e, n=n0, hsb=hsb0, t3=t3: e.tensor_tensor(
                                        t3[:, :], hsb[:, :],
                                        CCB[:, n * TS:(n + 1) * TS], ALU.mult))
                        if pend_add is not None:
                            p.inc_after("vector", "ve", 1,
                                        lambda e, t3p=pend_add: e.tensor_tensor(
                                            YA[:, :], YA[:, :], t3p[:, :], ALU.add))
                        pend_add = t3
                        vy += 1
                    p.wait("vector", "ve", scan_marks[gi1])
                    if gi0 + 2 < G:
                        emit_dbu(gi0 + 2)
                    if gi1 + 2 < G:
                        emit_dbu(gi1 + 2)
                    if n1 in GPN:
                        emit_gp_slice(n1, gi1, hsb1, dc)
                    else:
                        if dc == 0:
                            p.wait("vector", "se", cc_mark[n1])
                        p.wait("vector", "ve")
                        t3 = TS3B[vy % 2]
                        p.inc_after("vector", "ve", 1,
                                    lambda e, n=n1, hsb=hsb1, t3=t3: e.tensor_tensor(
                                        t3[:, :], hsb[:, :],
                                        CCB[:, n * TS:(n + 1) * TS], ALU.mult))
                        if pend_add is not None:
                            p.inc_after("vector", "ve", 1,
                                        lambda e, t3p=pend_add: e.tensor_tensor(
                                            YA[:, :], YA[:, :], t3p[:, :], ALU.add))
                        pend_add = t3
                        vy += 1
                if pend_add is not None:
                    p.wait("vector", "ve")
                    p.inc_after("vector", "ve", 1,
                                lambda e, t3p=pend_add: e.tensor_tensor(
                                    YA[:, :], YA[:, :], t3p[:, :], ALU.add))
                p.wait("vector", "gpm")
                p.wait("vector", "ve")
                p.inc_after("vector", "ve", 1,
                            lambda e: e.tensor_tensor(YA[:, :], YA[:, :], YAG[:, :],
                                                      ALU.add))
                p.wait("vector", "ve")
                p.inc_after("vector", "ve", 1,
                            lambda e, dc=dc: e.scalar_tensor_tensor(
                                YA[:, :], XCONV[:, dc * TS:(dc + 1) * TS], DM[:, dc:dc + 1],
                                YA[:, :], ALU.mult, ALU.add))
                p.wait("vector", "ve")
                p.wait("vector", "se", z_marks[16 + dc])
                p.inc_after("vector", "ve", 1,
                            lambda e, dc=dc: e.tensor_tensor(YF[:, dc * TS:(dc + 1) * TS],
                                                             YA[:, :],
                                                             ZS[:, dc * TS:(dc + 1) * TS],
                                                             ALU.mult))
                yf_marks[dc] = p.now("ve")
            emit_carry(G - 2)
            emit_carry(G - 1)
            prev_scan_marks = scan_marks
            prev_yf_marks = yf_marks
            prev_carry_mark = carry_marks[G - 1]
            ve_y = p.now("ve")
            p.wait("tensor", "ve", ve_y)
            for tt in range(TPS):
                if s > 0 or tt > 0:
                    p.wait("tensor", "ve")
                for half in range(2):
                    for fc in range(NDI):
                        fn = lambda e, tt=tt, half=half, fc=fc: e.matmul(
                            ps_o[:, half * 512:(half + 1) * 512],
                            YF[:, fc * TS + tt * 128: fc * TS + tt * 128 + 128],
                            WOP[:, fc * D + half * 512: fc * D + half * 512 + 512],
                            start=(fc == 0), stop=(fc == NDI - 1))
                        if fc == NDI - 1:
                            p.inc_after("tensor", "te", 1, fn)
                        else:
                            p.t(fn)
                te_o = p.now("te")
                p.wait("vector", "te", te_o)
                p.inc_after("vector", "ve", 1,
                            lambda e, tt=tt: e.scalar_tensor_tensor(
                                XS32[:, tt * D:(tt + 1) * D], XS32[:, tt * D:(tt + 1) * D],
                                2.0, ps_o[:, :], ALU.mult, ALU.add))
                p.wait("scalar", "st")
                p.wait("vector", "st")
                emit_ln_tile(p, XS32[:, tt * D:(tt + 1) * D], OUT[:, 0:D], stats,
                             sq_scr=OUT[:, 0:D])
                ve_out = p.now("ve")
                p.wait("sync", "ve", ve_out)
                p.inc_after("sync", "st", 16,
                            lambda e, s=s, tt=tt: e.dma_start(out=yo_t[s * TPS + tt],
                                                              in_=OUT[:, 0:D]))

        p.wait("sync", "st")
        p.barrier()
    if standalone:
        p.run()
        _octx.__exit__(None, None, None)
    return nc



def build_l2_attn(T_=None, p=None, dram=None):
    Tl = T_ or T
    NTl = Tl // 128
    NH = 8
    NQS = Tl // 512
    GEPS = 1e-5

    standalone = p is None
    if standalone:
        _nc = bass.Bass("TRN2", target_bir_lowering=False)
        _octx = ExitStack()
        _octx.__enter__()
        p = Prog(_nc, _octx)
        p.pfx = ""
        dram = lambda name, shape, dt, kind: _nc.dram_tensor(name, shape, dt, kind=kind).ap()
    nc = p.nc
    x = dram("x", [Tl, D], F32, "ExternalInput")
    wq = dram("wq", [128, NH * 128], F32, "ExternalInput")
    wk = dram("wk", [128, NH * 128], F32, "ExternalInput")
    wv = dram("wv", [128, NH * 128], F32, "ExternalInput")
    w1 = dram("w1", [D, D], F32, "ExternalInput")
    w2 = dram("w2", [D, D], F32, "ExternalInput")
    gnw = dram("gnw", [128, NH], F32, "ExternalInput")
    gnb = dram("gnb", [128, NH], F32, "ExternalInput")
    gp = dram("c_gp", [128, NH * 36], F32, "ExternalInput")
    ramp = dram("c_ramp", [128, NH * 512], F32, "ExternalInput")
    cmask = dram("c_mask", [128, 128], F32, "ExternalInput")
    ident = dram("c_ident", [128, 128], F32, "ExternalInput")
    yo = dram("y", [Tl, D], F32, "ExternalOutput")

    gammas = 1.0 - np.exp(np.linspace(np.log(1 / 512), np.log(1 / 32), NH))
    mm_max = [min(NTl, int(np.ceil(37.0 / (128 * -np.log(g)))) + 1) for g in gammas]

    x_t = x.rearrange("(nt p) d -> nt p d", p=128)
    yo_t = yo.rearrange("(nt p) d -> nt p d", p=128)

    with ExitStack() as ctx:
        sbt = lambda n, *a: ctx.enter_context(nc.sbuf_tensor(p.pfx + n, *a))
        pst = lambda n, *a: ctx.enter_context(nc.psum_tensor(p.pfx + n, *a))

        IDN = sbt("IDN", [128, 128], BF16)
        XF = sbt("XF2", [128, ND * Tl], BF16)
        RETF = sbt("RETF", [128, ND * Tl], BF16)
        WQ = sbt("WQ2", [128, NH * 128], BF16)
        WK = sbt("WK2", [128, NH * 128], BF16)
        WV = sbt("WV2", [128, NH * 128], BF16)
        KVW = sbt("KVW", [128, max(2 * Tl, ND * D)], BF16)
        W2S = sbt("W2S", [128, ND * D], BF16)
        W1S = KVW[:, 0:ND * D]
        GNW = sbt("GNW", [128, NH], F32)
        GNB = sbt("GNB", [128, NH], F32)
        GP = sbt("GP2", [128, NH * 36], F32)
        RAMP = sbt("RAMP", [128, NH * 512], BF16)
        MASK = sbt("MASK2", [128, 128], BF16)
        KF = KVW[:, 0:Tl]
        VT = KVW[:, Tl:2 * Tl]
        QS_ = sbt("QS2", [128, 512], BF16)
        PBJb = [sbt(f"PBJ2{i}", [128, 512], BF16) for i in range(2)]
        RET_s = sbt("RET_s", [128, 512], BF16)
        SQ = sbt("SQ2", [128, 512], BF16)
        RSC = sbt("RSC2f", [128, 512], F32)
        SG = sbt("SG2", [128, 512], BF16)
        S2F = sbt("S2F", [128, ND * 512], BF16)
        MU = S2F[:, :].bitcast(F32)[0:1, 0:512]
        RSTD = S2F[:, :].bitcast(F32)[0:1, 512:1024]
        ONE1 = sbt("ONE1f", [1, 128], F32)
        GMS = sbt("GMSrow", [1, 512], F32)
        ONEC = sbt("ONECb", [128, 1], BF16)
        SHARE = sbt("SHARE2", [128, 4 * D], BF16)
        XSB = SHARE[:, 0:2 * D]
        S32 = SHARE[:, :].bitcast(F32)
        OUT = sbt("OUT2", [128, D], F32)
        stats = alloc_stats(sbt)

        ps_tr = pst("ps_tr", [128, ND * 128], BF16)
        ps_a = pst("ps_a", [128, 512], F32)
        ps_scb = [pst(f"ps_sc{i}", [128, 512], F32) for i in range(2)]
        ps_sc = ps_scb[0]
        ps_ret = pst("ps_ret", [128, 512], F32)
        ps_rb = pst("ps_rb", [128, 512], F32)
        ps_po = pst("ps_po", [128, D], F32)

        p.inc_after("vector", "ve", 1, lambda e: e.memset(stats["eps"][:, 0:1], LN_EPS))
        p.inc_after("vector", "ve", 1, lambda e: e.memset(ONE1[:, :], 1.0))
        p.inc_after("vector", "ve", 1, lambda e: e.memset(ONEC[:, :], 1.0))
        gctr = {"step": 0}
        gse = {}
        gmsk = {}
        gevac = {}

        p.inc_after("gpsimd", "gld", 16, lambda e: e.dma_start(out=IDN[:, :], in_=ident))
        p.inc_after("gpsimd", "gld", 16, lambda e: e.dma_start(out=WQ[:, :], in_=wq))
        p.inc_after("gpsimd", "gld", 16, lambda e: e.dma_start(out=WK[:, :], in_=wk))
        p.inc_after("gpsimd", "gld", 16, lambda e: e.dma_start(out=WV[:, :], in_=wv))
        p.inc_after("gpsimd", "gld", 16,
                    lambda e: e.dma_start(out=W2S[:, :].rearrange("q (dc f) -> q dc f", f=D),
                                          in_=w2.rearrange("(dc q) f -> q dc f", q=128)))
        p.inc_after("gpsimd", "gld", 16, lambda e: e.dma_start(out=RAMP[:, :], in_=ramp))
        p.inc_after("gpsimd", "gld", 16, lambda e: e.dma_start(out=MASK[:, :], in_=cmask))
        p.inc_after("sync", "cld", 16, lambda e: e.dma_start(out=GNW[:, :], in_=gnw))
        p.inc_after("sync", "cld", 16, lambda e: e.dma_start(out=GNB[:, :], in_=gnb))
        p.inc_after("sync", "cld", 16, lambda e: e.dma_start(out=GP[:, :], in_=gp))
        cld_c = p.now("cld")

        te_nt = []
        for nt in range(NTl):
            buf = (nt % 2) * D
            par = f"gx{nt % 2}"
            if nt >= 2:
                p.wait("gpsimd", "te", te_nt[nt - 2])
            p.inc_after("gpsimd", par, 16,
                        lambda e, nt=nt, buf=buf: e.dma_start(out=XSB[:, buf:buf + D], in_=x_t[nt]))
            p.wait("tensor", par)
            if nt == 0:
                p.wait("tensor", "gld")
            p.wait("tensor", "ve")
            fns = [
                (lambda e, nt=nt, dc=dc, buf=buf: e.transpose(
                    ps_tr[:, dc * 128:(dc + 1) * 128],
                    XSB[:, buf + dc * 128: buf + (dc + 1) * 128], IDN[:, :]))
                for dc in range(ND)]
            for i, fn in enumerate(fns):
                if i == ND - 1:
                    p.inc_after("tensor", "te", 1, fn)
                else:
                    p.t(fn)
            te_nt.append(p.now("te"))
            p.wait("vector", "te")
            xf_view = XF[:, :].rearrange("q (nd t) -> q nd t", t=Tl)
            p.inc_after("vector", "ve", 1,
                        lambda e, nt=nt, xf_view=xf_view: e.tensor_copy(
                            xf_view[:, :, nt * 128:(nt + 1) * 128],
                            ps_tr[:, :].rearrange("q (nd f) -> q nd f", f=128)))
        ve_xf = p.now("ve")
        p.wait("tensor", "ve", ve_xf)
        p.wait("scalar", "cld", cld_c)

        for h in range(NH):
            for tc in range(Tl // 512):
                if h > 0 or tc > 0:
                    p.wait("tensor", "se")
                p.inc_after("tensor", "te", 1,
                            lambda e, h=h, tc=tc: e.matmul(
                                ps_a[:, :], WK[:, h * 128:(h + 1) * 128],
                                XF[:, h * Tl + tc * 512: h * Tl + tc * 512 + 512],
                                start=True, stop=True))
                p.wait("scalar", "te")
                p.inc_after("scalar", "se", 1,
                            lambda e, tc=tc: e.activation(KF[:, tc * 512:(tc + 1) * 512],
                                                          ps_a[:, :], AF.Copy))
            for kt in range(NTl):
                p.wait("tensor", "se")
                p.inc_after("tensor", "te", 1,
                            lambda e, h=h, kt=kt: e.matmul(
                                ps_a[:, 0:128], XF[:, h * Tl + kt * 128: h * Tl + kt * 128 + 128],
                                WV[:, h * 128:(h + 1) * 128], start=True, stop=True))
                p.wait("scalar", "te")
                p.inc_after("scalar", "se", 1,
                            lambda e, kt=kt: e.activation(VT[:, kt * 128:(kt + 1) * 128],
                                                          ps_a[:, 0:128], AF.Copy))
            se_kv = p.now("se")
            p.wait("tensor", "se", se_kv)

            QSb = [QS_, RET_s]
            ps_retb = [ps_ret, ps_po]

            def emit_qprep2(qs, h=h):
                qsb = QSb[qs % 2]
                p.wait("tensor", "ve", qst["psa"])
                p.inc_after("tensor", "te", 1,
                            lambda e, h=h, qs=qs: e.matmul(
                                ps_a[:, :], WQ[:, h * 128:(h + 1) * 128],
                                XF[:, h * Tl + qs * 512: h * Tl + qs * 512 + 512],
                                start=True, stop=True))
                tq = p.now("te")
                p.wait("vector", "te", tq)
                p.inc_after("vector", "ve", 1,
                            lambda e, h=h, qsb=qsb: e.tensor_tensor(
                                qsb[:, :], ps_a[:, :],
                                RAMP[:, h * 512:(h + 1) * 512], ALU.mult))
                qst["psa"] = p.now("ve")
                qst[qs] = qst["psa"]

            qst = {"psa": 0}
            emit_qprep2(0)
            for qs in range(NQS):
                qsb = QSb[qs % 2]
                ps_r = ps_retb[qs % 2]
                if qs + 1 < NQS:
                    emit_qprep2(qs + 1)
                kt_lo = max(0, qs * 4 - mm_max[h])
                kts = list(range(kt_lo, min(NTl, qs * 4 + 4)))
                se_exp = {}
                ve_msk = {}

                def emit_st_scale(i, kt, qs=qs, qsb=qsb, kts=kts, se_exp=se_exp,
                                  ve_msk=ve_msk):
                    gi = gctr["step"] + i
                    if gi >= 2:
                        p.wait("tensor", "se", gse[gi - 2])
                        if (gi - 2) in gmsk:
                            p.wait("tensor", "ve", gmsk[gi - 2])
                    if i == 0:
                        p.wait("tensor", "ve", qst[qs])
                    p.inc_after("tensor", "te", 1,
                                lambda e, kt=kt, gi=gi, qsb=qsb: e.matmul(
                                    ps_scb[gi % 2][:, :], KF[:, kt * 128:(kt + 1) * 128],
                                    qsb[:, :], start=True, stop=True))
                    te_s = p.now("te")
                    p.wait("scalar", "te", te_s)
                    mm = qs * 4 - kt
                    p.inc_after("scalar", "se", 1,
                                lambda e, h=h, mm=mm, gi=gi: e.activation(
                                    PBJb[gi % 2][:, :], ps_scb[gi % 2][:, :], AF.Copy,
                                    scale=GP[:, h * 36 + mm + 3: h * 36 + mm + 4]))
                    gse[gi] = p.now("se")
                    se_exp[i] = gse[gi]
                    if mm <= 0:
                        mt = -mm
                        p.wait("vector", "se", gse[gi])
                        if mt > 0:
                            p.inc_after("vector", "ve", 1,
                                        lambda e, gi=gi, mt=mt: e.memset(
                                            PBJb[gi % 2][:, 0:128 * mt], 0.0))
                        p.inc_after("vector", "ve", 1,
                                    lambda e, gi=gi, mt=mt: e.tensor_tensor(
                                        PBJb[gi % 2][:, 128 * mt:128 * mt + 128],
                                        PBJb[gi % 2][:, 128 * mt:128 * mt + 128],
                                        MASK[:, :], ALU.mult))
                        gmsk[gi] = p.now("ve")
                        ve_msk[i] = gmsk[gi]

                def emit_pv2(i, kt, last, qs=qs, kts=kts, se_exp=se_exp, ve_msk=ve_msk,
                             ps_r=ps_r, h=h):
                    gi = gctr["step"] + i
                    if i == 0:
                        p.wait("tensor", "ve", gevac.get(h * NQS + qs - 2, 0))
                    if qs * 4 - kt <= 0:
                        p.wait("tensor", "ve", ve_msk[i])
                    else:
                        p.wait("tensor", "se", se_exp[i])
                    st_flag = i == 0
                    fn = (lambda e, kt=kt, gi=gi, last=last, ps_r=ps_r, st_flag=st_flag:
                          e.matmul(
                        ps_r[:, 0:512], VT[:, kt * 128:(kt + 1) * 128], PBJb[gi % 2][:, :],
                        start=st_flag, stop=last))
                    if last:
                        p.inc_after("tensor", "te", 1, fn)
                    else:
                        p.t(fn)

                for i, kt in enumerate(kts):
                    emit_st_scale(i, kt)
                    if i > 0:
                        emit_pv2(i - 1, kts[i - 1], False)
                emit_pv2(len(kts) - 1, kts[-1], True)
                te_pv = p.now("te")
                gctr["step"] += len(kts)
                p.wait("vector", "te", te_pv)
                p.inc_after("vector", "ve", 1,
                            lambda e, h=h, qs=qs, ps_r=ps_r: e.tensor_copy(
                                RETF[:, h * Tl + qs * 512: h * Tl + qs * 512 + 512],
                                ps_r[:, 0:512]))
                gevac[h * NQS + qs] = p.now("ve")
        ve_ret0 = p.now("ve")

        SQb = [SQ, RET_s]
        MUb = [S2F[:, :].bitcast(F32)[0:1, 0:512], S2F[:, :].bitcast(F32)[0:1, 512:1024]]
        RSTDb = [S2F[:, :].bitcast(F32)[0:1, 1024:1536], S2F[:, :].bitcast(F32)[0:1, 1536:2048]]
        musum = [ps_scb[0], ps_scb[1]]
        sqsum = [ps_ret, ps_a]
        gn_se = {}
        gn_te = {}
        gn_ve = {}
        p.wait("scalar", "ve", ve_ret0)
        tiles = [(h, qs) for h in range(NH) for qs in range(NQS)]

        def gn_A(i):
            h, qs = tiles[i]
            tile = RETF[:, h * Tl + qs * 512: h * Tl + qs * 512 + 512]
            sq = SQb[i % 2]
            if i >= 2:
                p.wait("scalar", "te", gn_te[i - 2])
            p.inc_after("scalar", "se", 1,
                        lambda e, tile=tile, sq=sq: e.activation(sq[:, :], tile, AF.Square))
            gn_se[i] = p.now("se")
            if i >= 2:
                p.wait("tensor", "ve", gn_ve[i - 2])
            p.wait("tensor", "se", gn_se[i])
            p.inc_after("tensor", "te", 1,
                        lambda e, i=i, tile=tile: e.matmul(
                            musum[i % 2][0:1, :], ONEC[:, 0:1], tile, start=True, stop=True))
            p.inc_after("tensor", "te", 1,
                        lambda e, i=i, sq=sq: e.matmul(
                            sqsum[i % 2][0:1, :], ONEC[:, 0:1], sq[:, :], start=True, stop=True))
            gn_te[i] = p.now("te")

        def gn_B1(i):
            mu, rstd = MUb[i % 2], RSTDb[i % 2]
            p.wait("vector", "te", gn_te[i])
            p.wait("vector", "ve")
            p.inc_after("vector", "ve", 1,
                        lambda e, i=i, mu=mu: e.tensor_scalar_mul(mu, musum[i % 2][0:1, :],
                                                                  1.0 / 128))
            p.wait("vector", "ve")
            p.inc_after("vector", "ve", 1,
                        lambda e, i=i, rstd=rstd: e.tensor_scalar_mul(
                            rstd, sqsum[i % 2][0:1, :], 1.0 / 128))
            p.wait("vector", "ve")
            musq = GMS[0:1, 0:512]
            p.inc_after("vector", "ve", 1,
                        lambda e, mu=mu, musq=musq: e.tensor_mul(musq, mu, mu))
            p.wait("vector", "ve")
            p.inc_after("vector", "ve", 1,
                        lambda e, rstd=rstd, musq=musq: e.tensor_sub(rstd, rstd, musq))
            vm = p.now("ve")
            p.wait("scalar", "ve", vm)
            p.inc_after("scalar", "se", 1,
                        lambda e, rstd=rstd: e.activation(rstd, rstd, AF.Ln,
                                                          bias=stats["eps"][0:1, 0:1]))
            gn_ve[i] = p.now("ve")
            p.wait("scalar", "se")
            p.inc_after("scalar", "se", 1,
                        lambda e, rstd=rstd: e.activation(rstd, rstd, AF.Exp, scale=-0.5))
            gn_sm[i] = p.now("se")

        def gn_B2(i):
            h, qs = tiles[i]
            tile = RETF[:, h * Tl + qs * 512: h * Tl + qs * 512 + 512]
            mu, rstd = MUb[i % 2], RSTDb[i % 2]
            p.wait("tensor", "se", gn_sm[i])
            p.wait("tensor", "ve")
            p.inc_after("tensor", "te", 1,
                        lambda e, mu=mu: e.matmul(ps_po[:, 0:512], ONE1[0:1, :], mu,
                                                  start=True, stop=True))
            p.inc_after("tensor", "te", 1,
                        lambda e, rstd=rstd: e.matmul(ps_rb[:, :], ONE1[0:1, :], rstd,
                                                      start=True, stop=True))
            tb = p.now("te")
            p.wait("vector", "te", tb)
            p.inc_after("vector", "ve", 1,
                        lambda e, tile=tile: e.tensor_tensor(RSC[:, :], tile,
                                                             ps_po[:, 0:512], ALU.subtract))
            p.wait("vector", "ve")
            p.inc_after("vector", "ve", 1,
                        lambda e: e.tensor_tensor(RSC[:, :], RSC[:, :], ps_rb[:, :],
                                                  ALU.mult))
            p.wait("vector", "ve")
            p.inc_after("vector", "ve", 1,
                        lambda e, h=h, tile=tile: e.tensor_scalar(
                            tile, RSC[:, :], GNW[:, h:h + 1], GNB[:, h:h + 1],
                            ALU.mult, ALU.add))

        gn_sm = {}
        NTILE = len(tiles)
        for i in range(NTILE):
            gn_A(i)
            if i >= 1:
                gn_B1(i - 1)
            if i >= 2:
                gn_B2(i - 2)
        gn_B1(NTILE - 1)
        gn_B2(NTILE - 2)
        gn_B2(NTILE - 1)
        ve_ret = p.now("ve")

        p.wait("gpsimd", "te")
        p.inc_after("gpsimd", "gw1", 16,
                    lambda e: e.dma_start(out=W1S[0:128, 0:ND * D].rearrange("q (dc f) -> q dc f", f=D),
                                          in_=w1.rearrange("(dc q) f -> q dc f", q=128)))
        gw1 = p.now("gw1")
        p.wait("tensor", "gw1", gw1)

        p.wait("tensor", "ve", ve_ret)
        psW = [ps_a, ps_rb]
        SGb = [SG, SQ]
        for ts_i in range(NQS):
            w1_se = {}
            w1_ve = {}

            def emit_siluadd(fc, ts_i=ts_i):
                psx = psW[fc % 2]
                sg = SGb[fc % 2]
                if fc >= 2:
                    p.wait("scalar", "ve", w1_ve[fc - 2])
                p.inc_after("scalar", "se", 1,
                            lambda e, psx=psx, sg=sg: e.activation(sg[:, :], psx[:, :], AF.Silu))
                w1_se[fc] = p.now("se")
                p.wait("vector", "se", w1_se[fc])
                p.inc_after("vector", "ve", 1,
                            lambda e, ts_i=ts_i, fc=fc, sg=sg: e.tensor_tensor(
                                S2F[:, fc * 512:(fc + 1) * 512], sg[:, :],
                                RETF[:, fc * Tl + ts_i * 512: fc * Tl + ts_i * 512 + 512],
                                ALU.add))
                w1_ve[fc] = p.now("ve")

            for fc in range(ND):
                if fc >= 2:
                    p.wait("tensor", "se", w1_se[fc - 2])
                elif ts_i > 0:
                    p.wait("tensor", "se")
                fns = [
                    (lambda e, ts_i=ts_i, fc=fc, dc=dc: e.matmul(
                        psW[fc % 2][:, :], W1S[:, dc * D + fc * 128: dc * D + fc * 128 + 128],
                        XF[:, dc * Tl + ts_i * 512: dc * Tl + ts_i * 512 + 512],
                        start=(dc == 0), stop=(dc == ND - 1)))
                    for dc in range(ND)]
                for i, fn in enumerate(fns):
                    if i == ND - 1:
                        p.inc_after("tensor", "te", 1, fn)
                    else:
                        p.t(fn)
                p.wait("scalar", "te")
                if fc >= 1:
                    emit_siluadd(fc - 1)
            emit_siluadd(ND - 1)
            ve_s2 = p.now("ve")
            p.wait("tensor", "ve", ve_s2)
            for tt in range(4):
                nt = ts_i * 4 + tt
                p.wait("sync", "ve")
                p.inc_after("sync", "ld", 16,
                            lambda e, nt=nt: e.dma_start(out=S32[:, 0:D], in_=x_t[nt]))
                ld_nt = p.now("ld")
                if nt > 0:
                    p.wait("tensor", "ve")
                for half in range(2):
                    fns = [
                        (lambda e, tt=tt, half=half, fc=fc: e.matmul(
                            ps_po[:, half * 512:(half + 1) * 512],
                            S2F[:, fc * 512 + tt * 128: fc * 512 + tt * 128 + 128],
                            W2S[:, fc * D + half * 512: fc * D + half * 512 + 512],
                            start=(fc == 0), stop=(fc == ND - 1)))
                        for fc in range(ND)]
                    for i, fn in enumerate(fns):
                        if i == ND - 1:
                            p.inc_after("tensor", "te", 1, fn)
                        else:
                            p.t(fn)
                te_po = p.now("te")
                p.wait("vector", "te", te_po)
                p.wait("vector", "ld", ld_nt)
                p.inc_after("vector", "ve", 1,
                            lambda e: e.tensor_tensor(S32[:, 0:D], S32[:, 0:D],
                                                      ps_po[:, :], ALU.add))
                p.wait("scalar", "st")
                p.wait("vector", "st")
                emit_ln_tile(p, S32[:, 0:D], OUT[:, 0:D], stats, sq_scr=OUT[:, 0:D])
                ve_out = p.now("ve")
                p.wait("sync", "ve", ve_out)
                p.inc_after("sync", "st", 16,
                            lambda e, nt=nt: e.dma_start(out=yo_t[nt], in_=OUT[:, 0:D]))

        p.wait("sync", "st")
        p.barrier()
    if standalone:
        p.run()
        _octx.__exit__(None, None, None)
    return nc



def build_l2_ffn(T_=None, p=None, dram=None):
    Tl = T_ or T
    NTl = Tl // 128
    H = 4096
    NHC = H // 128
    TS = 256
    NS = Tl // TS
    TPS = TS // 128
    REPS = 1e-8

    standalone = p is None
    if standalone:
        _nc = bass.Bass("TRN2", target_bir_lowering=False)
        _octx = ExitStack()
        _octx.__enter__()
        p = Prog(_nc, _octx)
        p.pfx = ""
        dram = lambda name, shape, dt, kind: _nc.dram_tensor(name, shape, dt, kind=kind).ap()
    nc = p.nc
    x = dram("x", [Tl, D], F32, "ExternalInput")
    wupT = dram("wupT", [D, H], F32, "ExternalInput")
    wdnT = dram("wdnT", [H, D], F32, "ExternalInput")
    ident = dram("c_ident", [128, 128], F32, "ExternalInput")
    yo = dram("y", [Tl, D], F32, "ExternalOutput")

    x_t = x.rearrange("(nt p) d -> nt p d", p=128)
    yo_t = yo.rearrange("(nt p) d -> nt p d", p=128)

    with ExitStack() as ctx:
        sbt = lambda n, *a: ctx.enter_context(nc.sbuf_tensor(p.pfx + n, *a))
        pst = lambda n, *a: ctx.enter_context(nc.psum_tensor(p.pfx + n, *a))

        IDN = sbt("IDN", [128, 128], BF16)
        WUQ = sbt("WUQ", [128, ND * H], BF16)
        WDQ = sbt("WDQ", [128, NHC * D], BF16)
        WSTG = sbt("WSTG2", [128, 4096], F32)
        ABSS = sbt("ABSS", [128, 2], F32)
        TH = sbt("TH2", [128, 2], F32)
        SGN = sbt("SGN2", [128, 4096], BF16)
        AB2 = sbt("AB2f", [128, 4096], F32)
        RED = sbt("RED2", [128, 1], F32)
        XNT = sbt("XNT2", [128, TPS * D], BF16)
        XS32 = sbt("XS32f2", [128, TPS * D], F32)
        XNF = [sbt(f"XNF{i}", [128, ND * TS], BF16) for i in range(2)]
        HSb = [sbt(f"HS2{i}", [128, TS], BF16) for i in range(2)]
        SQHb = [sbt(f"SQH2{i}", [128, TS], BF16) for i in range(2)]
        RSTDC = sbt("RSTDC", [128, TPS], F32)
        ONEC = sbt("ONECf2", [128, 1], BF16)
        ONE1 = sbt("ONE1f2", [1, 128], F32)
        OUT = sbt("OUT2f", [128, D], F32)
        RROW = sbt("RROW", [1, TS], F32)
        stats = alloc_stats(sbt)

        ps_tr = pst("ps_tr", [128, ND * 128], BF16)
        ps_ub = [pst(f"ps_u{i}", [128, TS], F32) for i in range(2)]
        ps_sq = pst("ps_sq", [1, TS], F32)
        ps_d = [pst(f"ps_d{i}", [128, D], F32) for i in range(TPS)]

        p.inc_after("vector", "ve", 1, lambda e: e.memset(stats["eps"][:, 0:1], LN_EPS))
        p.inc_after("vector", "ve", 1, lambda e: e.memset(ONEC[:, :], 1.0))
        p.inc_after("vector", "ve", 1, lambda e: e.memset(ONE1[:, :], 1.0))
        p.inc_after("vector", "ve", 1, lambda e: e.memset(ABSS[:, :], 0.0))
        p.inc_after("gpsimd", "gld", 16, lambda e: e.dma_start(out=IDN[:, :], in_=ident))

        wup_r = wupT.rearrange("(rc q) f -> rc q f", q=128)
        wdn_r = wdnT.rearrange("(rc q) f -> rc q f", q=128)
        ONECF = sbt("ONECF", [128, 1], F32)
        THR = sbt("THR2", [1, 2], F32)
        p.inc_after("vector", "ve", 1, lambda e: e.memset(ONECF[:, :], 1.0))

        def tern_matrix(wi_, wr, nrc, ncol, dest):
            halves = []
            for rc in range(nrc):
                if ncol > 2048:
                    halves += [(rc, 0, 2048), (rc, 2048, ncol - 2048)]
                else:
                    halves += [(rc, 0, ncol)]
            p.wait("sync", "ve")
            p.wait("sync", "se")
            wbase = [p.now("wld0"), p.now("wld1")]
            marks = {}
            for ci, (rc, c0, cw) in enumerate(halves):
                par = ci % 2
                stg = WSTG[:, par * 2048: par * 2048 + cw]
                if ci >= 2:
                    p.wait("sync", "ve", marks[ci - 2])
                p.inc_after("sync", "wld" + str(par), 16,
                            lambda e, wr=wr, rc=rc, c0=c0, cw=cw, stg=stg: e.dma_start(
                                out=stg, in_=wr[rc][:, c0:c0 + cw]))
                p.wait("vector", "wld" + str(par), wbase[par] + 16 * (ci // 2 + 1))
                p.wait("vector", "ve")
                p.inc_after("vector", "ve", 1,
                            lambda e, stg=stg: e.reduce_sum(RED[:, 0:1], stg, axis=AX.X,
                                                            apply_absolute_value=True))
                p.wait("vector", "ve")
                p.inc_after("vector", "ve", 1,
                            lambda e, wi_=wi_: e.tensor_add(ABSS[:, wi_:wi_ + 1],
                                                            ABSS[:, wi_:wi_ + 1],
                                                            RED[:, 0:1]))
                marks[ci] = p.now("ve")
            p.wait("tensor", "ve")
            p.inc_after("tensor", "te", 1,
                        lambda e, wi_=wi_: e.matmul(ps_d[0][0:1, wi_:wi_ + 1], ONECF[:, 0:1],
                                                    ABSS[:, wi_:wi_ + 1],
                                                    start=True, stop=True))
            te_al = p.now("te")
            p.wait("vector", "te", te_al)
            p.inc_after("vector", "ve", 1,
                        lambda e, wi_=wi_: e.tensor_scalar_mul(
                            THR[0:1, wi_:wi_ + 1], ps_d[0][0:1, wi_:wi_ + 1],
                            0.5 / (D * H)))
            p.wait("tensor", "ve")
            p.inc_after("tensor", "te", 1,
                        lambda e, wi_=wi_: e.matmul(ps_d[0][:, 4 + wi_:5 + wi_], ONE1[0:1, :],
                                                    THR[0:1, wi_:wi_ + 1],
                                                    start=True, stop=True))
            te_th = p.now("te")
            p.wait("vector", "te", te_th)
            p.inc_after("vector", "ve", 1,
                        lambda e, wi_=wi_: e.tensor_copy(TH[:, wi_:wi_ + 1],
                                                         ps_d[0][:, 4 + wi_:5 + wi_]))
            ve_th = p.now("ve")
            p.wait("sync", "ve")
            p.wait("sync", "se")
            wbase = [p.now("wld0"), p.now("wld1")]
            bmarks = {}
            for ci, (rc, c0, cw) in enumerate(halves):
                par = ci % 2
                stg = WSTG[:, par * 2048: par * 2048 + cw]
                sgn = SGN[:, par * 2048: par * 2048 + cw]
                ab = AB2[:, par * 2048: par * 2048 + cw]
                if ci >= 2:
                    p.wait("sync", "ve", bmarks[ci - 2][0])
                    p.wait("sync", "se", bmarks[ci - 2][1])
                p.inc_after("sync", "wld" + str(par), 16,
                            lambda e, wr=wr, rc=rc, c0=c0, cw=cw, stg=stg: e.dma_start(
                                out=stg, in_=wr[rc][:, c0:c0 + cw]))
                p.wait("scalar", "wld" + str(par), wbase[par] + 16 * (ci // 2 + 1))
                p.inc_after("scalar", "se", 1,
                            lambda e, stg=stg, sgn=sgn: e.activation(sgn, stg, AF.Sign))
                p.wait("vector", "wld" + str(par), wbase[par] + 16 * (ci // 2 + 1))
                p.inc_after("vector", "ve", 1,
                            lambda e, stg=stg, ab=ab: e.scalar_tensor_tensor(
                                ab, stg, -1.0, stg, ALU.mult, ALU.max))
                if wi_ == 0:
                    out_ap = dest[:, :].rearrange("q (dc f) -> q dc f", f=H)[:, rc, c0:c0 + cw]
                else:
                    out_ap = dest[:, rc * D + c0: rc * D + c0 + cw]
                p.wait("vector", "se")
                p.wait("vector", "ve")
                p.inc_after("vector", "ve", 1,
                            lambda e, out_ap=out_ap, ab=ab, sgn=sgn, wi_=wi_:
                            e.scalar_tensor_tensor(out_ap, ab, TH[:, wi_:wi_ + 1], sgn,
                                                   ALU.is_gt, ALU.mult))
                bmarks[ci] = (p.now("ve"), p.now("se"))

        tern_matrix(0, wup_r, ND, H, WUQ)
        wuq_ready = p.now("ve")
        tern_matrix(1, wdn_r, NHC, D, WDQ)
        wdq_ready = p.now("ve")

        wuq_view = WUQ[:, :].rearrange("q (dc hc f) -> q dc hc f", hc=NHC, f=128)
        for s in range(NS):
            xnf = XNF[s % 2]
            if s > 0:
                p.wait("sync", "ve")
            for tt in range(TPS):
                p.inc_after("sync", "ld", 16,
                            lambda e, s=s, tt=tt: e.dma_start(out=XS32[:, tt * D:(tt + 1) * D],
                                                              in_=x_t[s * TPS + tt]))
            ld_s = p.now("ld")
            for tt in range(TPS):
                ssq, rstd = stats["ssq"], stats["rstd"]
                p.wait("scalar", "ld", ld_s)
                p.wait("scalar", "st")
                p.wait("scalar", "ve")
                p.inc_after("scalar", "se", 1,
                            lambda e, tt=tt: e.activation(OUT[:, 0:D],
                                                          XS32[:, tt * D:(tt + 1) * D],
                                                          AF.Square, accum_out=ssq[:, 0:1]))
                p.wait("scalar", "se")
                p.inc_after("scalar", "se", 1,
                            lambda e: e.activation(rstd[:, 0:1], ssq[:, 0:1], AF.Ln,
                                                   scale=1.0 / D))
                p.wait("scalar", "se")
                p.inc_after("scalar", "se", 1,
                            lambda e: e.activation(rstd[:, 0:1], rstd[:, 0:1], AF.Exp,
                                                   scale=0.5))
                p.wait("vector", "se")
                p.inc_after("vector", "ve", 1,
                            lambda e: e.tensor_scalar_add(rstd[:, 0:1], rstd[:, 0:1], REPS))
                p.wait("vector", "ve")
                p.inc_after("vector", "ve", 1,
                            lambda e: e.reciprocal(rstd[:, 0:1], rstd[:, 0:1]))
                p.wait("vector", "ve")
                p.inc_after("vector", "ve", 1,
                            lambda e, tt=tt: e.tensor_scalar_mul(
                                XNT[:, tt * D:(tt + 1) * D], XS32[:, tt * D:(tt + 1) * D],
                                rstd[:, 0:1]))
            ve_xn = p.now("ve")
            p.wait("tensor", "ve", ve_xn)
            p.wait("tensor", "gld")
            xnf_view = xnf[:, :].rearrange("q (nd t) -> q nd t", t=TS)
            for tt in range(TPS):
                p.wait("tensor", "ve")
                fns = [
                    (lambda e, tt=tt, dc=dc: e.transpose(
                        ps_tr[:, dc * 128:(dc + 1) * 128],
                        XNT[:, tt * D + dc * 128: tt * D + (dc + 1) * 128], IDN[:, :]))
                    for dc in range(ND)]
                for i, fn in enumerate(fns):
                    if i == ND - 1:
                        p.inc_after("tensor", "te", 1, fn)
                    else:
                        p.t(fn)
                p.wait("vector", "te")
                p.inc_after("vector", "ve", 1,
                            lambda e, tt=tt, xnf_view=xnf_view: e.tensor_copy(
                                xnf_view[:, :, tt * 128:(tt + 1) * 128],
                                ps_tr[:, :].rearrange("q (nd f) -> q nd f", f=128)))
            ve_xf = p.now("ve")
            p.wait("tensor", "ve", ve_xf)

            se_h = {}
            te_dn_marks = {}

            def emit_down2(hc):
                if s == 0 and hc == 0:
                    p.wait("tensor", "ve", wdq_ready)
                p.wait("tensor", "se", se_h[hc])
                p.inc_after("tensor", "te", 1,
                            lambda e, hc=hc: e.matmul(ps_sq[0:1, :], ONEC[:, 0:1],
                                                      SQHb[hc % 2][:, :],
                                                      start=(hc == 0), stop=(hc == NHC - 1)))
                for tt in range(TPS):
                    for half in range(2):
                        fn = lambda e, hc=hc, tt=tt, half=half: e.matmul(
                            ps_d[tt][:, half * 512:(half + 1) * 512],
                            HSb[hc % 2][:, tt * 128:(tt + 1) * 128],
                            WDQ[:, hc * D + half * 512: hc * D + half * 512 + 512],
                            start=(hc == 0), stop=(hc == NHC - 1))
                        if tt == TPS - 1 and half == 1:
                            p.inc_after("tensor", "te", 1, fn)
                        else:
                            p.t(fn)
                te_dn_marks[hc] = p.now("te")

            for hc in range(NHC):
                if s == 0 and hc == 0:
                    p.wait("tensor", "ve", wuq_ready)
                if hc >= 2:
                    p.wait("tensor", "se", se_h[hc - 2] - 1)
                fns = [
                    (lambda e, hc=hc, dc=dc, xnf=xnf: e.matmul(
                        ps_ub[hc % 2][:, :], wuq_view[:, dc, hc, :],
                        xnf[:, dc * TS:(dc + 1) * TS],
                        start=(dc == 0), stop=(dc == ND - 1)))
                    for dc in range(ND)]
                for i, fn in enumerate(fns):
                    if i == ND - 1:
                        p.inc_after("tensor", "te", 1, fn)
                    else:
                        p.t(fn)
                te_u = p.now("te")
                if hc >= 1:
                    emit_down2(hc - 1)
                p.wait("scalar", "te", te_u)
                if hc >= 2:
                    p.wait("scalar", "te", te_dn_marks[hc - 2])
                p.inc_after("scalar", "se", 1,
                            lambda e, hc=hc: e.activation(HSb[hc % 2][:, :],
                                                          ps_ub[hc % 2][:, :], AF.Gelu))
                p.wait("scalar", "se")
                p.inc_after("scalar", "se", 1,
                            lambda e, hc=hc: e.activation(SQHb[hc % 2][:, :],
                                                          HSb[hc % 2][:, :], AF.Square))
                se_h[hc] = p.now("se")
            emit_down2(NHC - 1)
            te_dn = p.now("te")
            p.wait("vector", "te", te_dn)
            p.inc_after("vector", "ve", 1,
                        lambda e: e.tensor_scalar_mul(RROW[0:1, :], ps_sq[0:1, :], 1.0 / H))
            p.wait("scalar", "ve")
            p.inc_after("scalar", "se", 1,
                        lambda e: e.activation(RROW[0:1, :], RROW[0:1, :], AF.Ln))
            p.wait("scalar", "se")
            p.inc_after("scalar", "se", 1,
                        lambda e: e.activation(RROW[0:1, :], RROW[0:1, :], AF.Exp, scale=0.5))
            p.wait("vector", "se")
            p.inc_after("vector", "ve", 1,
                        lambda e: e.tensor_scalar_add(RROW[0:1, :], RROW[0:1, :], REPS))
            p.wait("scalar", "ve")
            p.inc_after("scalar", "se", 1,
                        lambda e: e.activation(RROW[0:1, :], RROW[0:1, :], AF.Ln))
            p.wait("scalar", "se")
            p.inc_after("scalar", "se", 1,
                        lambda e: e.activation(RROW[0:1, :], RROW[0:1, :], AF.Exp, scale=-1.0))
            ve_rr = p.now("se")
            p.wait("tensor", "se", ve_rr)
            for tt in range(TPS):
                p.inc_after("tensor", "te", 1,
                            lambda e, tt=tt: e.matmul(
                                ps_ub[0][:, tt:tt + 1],
                                RROW[0:1, tt * 128:(tt + 1) * 128], ONE1[0:1, 0:1],
                                start=True, stop=True))
            te_rt = p.now("te")
            p.wait("vector", "te", te_rt)
            p.inc_after("vector", "ve", 1,
                        lambda e: e.tensor_copy(RSTDC[:, 0:TPS], ps_ub[0][:, 0:TPS]))
            ve_rt = p.now("ve")

            for tt in range(TPS):
                p.wait("vector", "te", te_dn)
                p.wait("vector", "ve", ve_rt)
                p.wait("vector", "ld", ld_s)
                p.inc_after("vector", "ve", 1,
                            lambda e, tt=tt: e.scalar_tensor_tensor(
                                XS32[:, tt * D:(tt + 1) * D], ps_d[tt][:, :],
                                RSTDC[:, tt:tt + 1], XS32[:, tt * D:(tt + 1) * D],
                                ALU.mult, ALU.add))
                p.wait("scalar", "st")
                p.wait("vector", "st")
                emit_ln_tile(p, XS32[:, tt * D:(tt + 1) * D], OUT[:, 0:D], stats,
                             sq_scr=OUT[:, 0:D])
                ve_out = p.now("ve")
                p.wait("sync", "ve", ve_out)
                p.inc_after("sync", "st", 16,
                            lambda e, s=s, tt=tt: e.dma_start(out=yo_t[s * TPS + tt],
                                                              in_=OUT[:, 0:D]))

        p.wait("sync", "st")
        p.barrier()
    if standalone:
        p.run()
        _octx.__exit__(None, None, None)
    return nc



def build_l3_ffn(T_=None, p=None, dram=None):
    Tl = T_ or T
    NTl = Tl // 128
    NQS = Tl // 512

    standalone = p is None
    if standalone:
        _nc = bass.Bass("TRN2", target_bir_lowering=False)
        _octx = ExitStack()
        _octx.__enter__()
        p = Prog(_nc, _octx)
        p.pfx = ""
        dram = lambda name, shape, dt, kind: _nc.dram_tensor(name, shape, dt, kind=kind).ap()
    nc = p.nc
    x = dram("x", [Tl, D], F32, "ExternalInput")
    wrT = dram("wrT", [D, D], F32, "ExternalInput")
    wkT = dram("wkT", [D, D], F32, "ExternalInput")
    wvT = dram("wvT", [D, D], F32, "ExternalInput")
    mur = dram("mur", [128, ND], F32, "ExternalInput")
    muk = dram("muk", [128, ND], F32, "ExternalInput")
    ident = dram("c_ident", [128, 128], F32, "ExternalInput")
    yo = dram("y", [Tl, D], F32, "ExternalOutput")

    x_t = x.rearrange("(nt p) d -> nt p d", p=128)
    yo_t = yo.rearrange("(nt p) d -> nt p d", p=128)

    with ExitStack() as ctx:
        sbt = lambda n, *a: ctx.enter_context(nc.sbuf_tensor(p.pfx + n, *a))
        pst = lambda n, *a: ctx.enter_context(nc.psum_tensor(p.pfx + n, *a))

        IDN = sbt("IDN", [128, 128], BF16)
        XF = sbt("XF3", [128, ND * Tl], BF16)
        WRT = sbt("WRT", [128, ND * D], BF16)
        WKT = sbt("WKT", [128, ND * D], BF16)
        WVT = sbt("WVT", [128, ND * D], BF16)
        MUR = sbt("MUR", [128, ND], F32)
        MUK = sbt("MUK", [128, ND], F32)
        OMR = sbt("OMR", [128, ND], F32)
        OMK = sbt("OMK", [128, ND], F32)
        XRFb = [sbt(f"XRF{i}", [128, ND * 512], BF16) for i in range(2)]
        XKFb = [sbt(f"XKF{i}", [128, ND * 512], BF16) for i in range(2)]
        KA = sbt("KA", [128, ND * 512], BF16)
        RT = sbt("RT", [128, 4 * D], BF16)
        RSC = sbt("RSC3", [128, 512], F32)
        KSC = sbt("KSC3", [128, 512], F32)
        SHARE = sbt("SHARE3", [128, 4 * D], BF16)
        XSB = SHARE[:, 0:2 * D]
        S32 = SHARE[:, :].bitcast(F32)
        OUT = sbt("OUT3", [128, D], F32)
        stats = alloc_stats(sbt)

        ps_tr = pst("ps_tr", [128, ND * 128], BF16)
        ps_ab = [pst(f"ps_a{i}", [128, 512], F32) for i in range(2)]
        ps_a = ps_ab[0]
        ps_pob = [pst(f"ps_po{i}", [128, D], F32) for i in range(2)]
        ps_po = ps_pob[0]

        p.inc_after("vector", "ve", 1, lambda e: e.memset(stats["eps"][:, 0:1], LN_EPS))

        p.inc_after("gpsimd", "gld", 16, lambda e: e.dma_start(out=IDN[:, :], in_=ident))
        p.inc_after("gpsimd", "gld", 16,
                    lambda e: e.dma_start(out=WRT[:, :].rearrange("q (dc f) -> q dc f", f=D),
                                          in_=wrT.rearrange("(dc q) f -> q dc f", q=128)))
        p.inc_after("gpsimd", "gld", 16,
                    lambda e: e.dma_start(out=WKT[:, :].rearrange("q (dc f) -> q dc f", f=D),
                                          in_=wkT.rearrange("(dc q) f -> q dc f", q=128)))
        p.inc_after("gpsimd", "gld", 16,
                    lambda e: e.dma_start(out=WVT[:, :].rearrange("q (dc f) -> q dc f", f=D),
                                          in_=wvT.rearrange("(dc q) f -> q dc f", q=128)))
        p.inc_after("sync", "cld", 16, lambda e: e.dma_start(out=MUR[:, :], in_=mur))
        p.inc_after("sync", "cld", 16, lambda e: e.dma_start(out=MUK[:, :], in_=muk))
        cld_c = p.now("cld")
        p.wait("vector", "cld", cld_c)
        p.inc_after("vector", "ve", 1, lambda e: e.tensor_scalar(OMR[:, :], MUR[:, :], -1.0, 1.0,
                                                                 ALU.mult, ALU.add))
        p.inc_after("vector", "ve", 1, lambda e: e.tensor_scalar(OMK[:, :], MUK[:, :], -1.0, 1.0,
                                                                 ALU.mult, ALU.add))

        te_nt = []
        for nt in range(NTl):
            buf = (nt % 2) * D
            par = f"gx{nt % 2}"
            if nt >= 2:
                p.wait("gpsimd", "te", te_nt[nt - 2])
            p.inc_after("gpsimd", par, 16,
                        lambda e, nt=nt, buf=buf: e.dma_start(out=XSB[:, buf:buf + D], in_=x_t[nt]))
            p.wait("tensor", par)
            if nt == 0:
                p.wait("tensor", "gld")
            p.wait("tensor", "ve")
            fns = [
                (lambda e, nt=nt, dc=dc, buf=buf: e.transpose(
                    ps_tr[:, dc * 128:(dc + 1) * 128],
                    XSB[:, buf + dc * 128: buf + (dc + 1) * 128], IDN[:, :]))
                for dc in range(ND)]
            for i, fn in enumerate(fns):
                if i == ND - 1:
                    p.inc_after("tensor", "te", 1, fn)
                else:
                    p.t(fn)
            te_nt.append(p.now("te"))
            p.wait("vector", "te")
            xf_view = XF[:, :].rearrange("q (nd t) -> q nd t", t=Tl)
            p.inc_after("vector", "ve", 1,
                        lambda e, nt=nt, xf_view=xf_view: e.tensor_copy(
                            xf_view[:, :, nt * 128:(nt + 1) * 128],
                            ps_tr[:, :].rearrange("q (nd f) -> q nd f", f=128)))
        ve_xf = p.now("ve")
        p.wait("tensor", "ve", ve_xf)

        slab_te_end = {}
        mix_marks = {}

        def emit_shift(qs):
            t0 = qs * 512
            XRF = XRFb[qs % 2]
            XKF = XKFb[qs % 2]
            p.wait("vector", "ve", ve_xf)
            p.wait("vector", "te", slab_te_end.get(qs - 2, 0))
            for dc in range(ND):
                for (XT, MUc, OMc) in ((XRF, MUR, OMR), (XKF, MUK, OMK)):
                    p.inc_after("vector", "ve", 1,
                                lambda e, dc=dc, XT=XT, MUc=MUc, t0=t0: e.tensor_scalar(
                                    RSC[:, :], XF[:, dc * Tl + t0: dc * Tl + t0 + 512],
                                    MUc[:, dc:dc + 1], None, ALU.mult))
                    p.wait("vector", "ve")
                    if t0 == 0:
                        p.inc_after("vector", "ve", 1,
                                    lambda e, dc=dc, XT=XT, OMc=OMc: e.scalar_tensor_tensor(
                                        XT[:, dc * 512 + 1: dc * 512 + 512],
                                        XF[:, dc * Tl: dc * Tl + 511],
                                        OMc[:, dc:dc + 1], RSC[:, 1:512],
                                        ALU.mult, ALU.add))
                        p.wait("vector", "ve")
                        p.inc_after("vector", "ve", 1,
                                    lambda e, dc=dc, XT=XT: e.tensor_copy(
                                        XT[:, dc * 512: dc * 512 + 1], RSC[:, 0:1]))
                    else:
                        p.inc_after("vector", "ve", 1,
                                    lambda e, dc=dc, XT=XT, OMc=OMc, t0=t0: e.scalar_tensor_tensor(
                                        XT[:, dc * 512: dc * 512 + 512],
                                        XF[:, dc * Tl + t0 - 1: dc * Tl + t0 + 511],
                                        OMc[:, dc:dc + 1], RSC[:, :],
                                        ALU.mult, ALU.add))
                    p.wait("vector", "ve")
            mix_marks[qs] = p.now("ve")

        emit_shift(0)
        for qs in range(NQS):
            XRF = XRFb[qs % 2]
            XKF = XKFb[qs % 2]
            p.wait("tensor", "ve", mix_marks[qs])

            r_te = {}
            r_se = {}

            def emit_sig(ri):
                tt, half = divmod(ri, 2)
                p.wait("scalar", "te", r_te[ri])
                p.inc_after("scalar", "se", 1,
                            lambda e, tt=tt, half=half, ri=ri: e.activation(
                                RT[:, tt * D + half * 512: tt * D + half * 512 + 512],
                                ps_ab[ri % 2][:, :], AF.Sigmoid))
                r_se[ri] = p.now("se")

            for ri in range(8):
                tt, half = divmod(ri, 2)
                if ri >= 2:
                    p.wait("tensor", "se", r_se[ri - 2])
                elif qs > 0:
                    p.wait("tensor", "se")
                fns = [
                    (lambda e, tt=tt, half=half, dc=dc, XRF=XRF, ri=ri: e.matmul(
                        ps_ab[ri % 2][:, :],
                        XRF[:, dc * 512 + tt * 128: dc * 512 + tt * 128 + 128],
                        WRT[:, dc * D + half * 512: dc * D + half * 512 + 512],
                        start=(dc == 0), stop=(dc == ND - 1)))
                    for dc in range(ND)]
                for i, fn in enumerate(fns):
                    if i == ND - 1:
                        p.inc_after("tensor", "te", 1, fn)
                    else:
                        p.t(fn)
                r_te[ri] = p.now("te")
                if ri >= 1:
                    emit_sig(ri - 1)
            emit_sig(7)
            k_te = {}
            k_ve = {}
            k_se = {}
            KSCb = [KSC, OUT[:, 0:512]]

            def emit_ksq(fc):
                p.wait("vector", "te", k_te[fc])
                if fc % 2 == 1:
                    p.wait("vector", "st")
                if fc >= 2:
                    p.wait("vector", "se", k_se[fc - 2])
                p.inc_after("vector", "ve", 1,
                            lambda e, fc=fc: e.tensor_scalar_max(
                                KSCb[fc % 2][:, :], ps_ab[fc % 2][:, :], 0.0))
                k_ve[fc] = p.now("ve")
                p.wait("scalar", "ve", k_ve[fc])
                p.inc_after("scalar", "se", 1,
                            lambda e, fc=fc: e.activation(KA[:, fc * 512:(fc + 1) * 512],
                                                          KSCb[fc % 2][:, :], AF.Square))
                k_se[fc] = p.now("se")

            for fc in range(ND):
                if fc >= 2:
                    p.wait("tensor", "ve", k_ve[fc - 2])
                else:
                    p.wait("tensor", "se", r_se[6 + fc])
                fns = [
                    (lambda e, fc=fc, dc=dc, XKF=XKF: e.matmul(
                        ps_ab[fc % 2][:, :], WKT[:, dc * D + fc * 128: dc * D + fc * 128 + 128],
                        XKF[:, dc * 512:(dc + 1) * 512],
                        start=(dc == 0), stop=(dc == ND - 1)))
                    for dc in range(ND)]
                for i, fn in enumerate(fns):
                    if i == ND - 1:
                        p.inc_after("tensor", "te", 1, fn)
                    else:
                        p.t(fn)
                k_te[fc] = p.now("te")
                if fc >= 1:
                    emit_ksq(fc - 1)
            emit_ksq(ND - 1)
            se_ka = p.now("se")
            if qs + 1 < NQS:
                emit_shift(qs + 1)
            p.wait("tensor", "se", se_ka)

            po_ve3 = {}
            for tt in range(4):
                nt = qs * 4 + tt
                p.wait("sync", "ve")
                p.inc_after("sync", "ld", 16,
                            lambda e, nt=nt, tt=tt: e.dma_start(
                                out=S32[:, (tt % 2) * D:(tt % 2) * D + D], in_=x_t[nt]))
                ld_nt = p.now("ld")
                if tt >= 2:
                    p.wait("tensor", "ve", po_ve3[tt - 2])
                elif qs > 0:
                    p.wait("tensor", "ve")
                for half in range(2):
                    fns = [
                        (lambda e, tt=tt, half=half, fc=fc: e.matmul(
                            ps_pob[tt % 2][:, half * 512:(half + 1) * 512],
                            KA[:, fc * 512 + tt * 128: fc * 512 + tt * 128 + 128],
                            WVT[:, fc * D + half * 512: fc * D + half * 512 + 512],
                            start=(fc == 0), stop=(fc == ND - 1)))
                        for fc in range(ND)]
                    for i, fn in enumerate(fns):
                        if i == ND - 1:
                            p.inc_after("tensor", "te", 1, fn)
                        else:
                            p.t(fn)
                te_po = p.now("te")
                p.wait("vector", "te", te_po)
                p.wait("vector", "ld", ld_nt)
                p.wait("vector", "se")
                p.wait("vector", "st")
                p.wait("scalar", "st")
                p.inc_after("vector", "ve", 1,
                            lambda e, tt=tt: e.tensor_tensor(
                                OUT[:, 0:D], RT[:, tt * D:(tt + 1) * D],
                                ps_pob[tt % 2][:, :], ALU.mult))
                p.wait("vector", "ve")
                p.inc_after("vector", "ve", 1,
                            lambda e, tt=tt: e.tensor_tensor(
                                S32[:, (tt % 2) * D:(tt % 2) * D + D],
                                S32[:, (tt % 2) * D:(tt % 2) * D + D],
                                OUT[:, 0:D], ALU.add))
                po_ve3[tt] = p.now("ve")
                emit_ln_tile(p, S32[:, (tt % 2) * D:(tt % 2) * D + D], OUT[:, 0:D], stats,
                             sq_scr=OUT[:, 0:D])
                ve_out = p.now("ve")
                p.wait("sync", "ve", ve_out)
                p.inc_after("sync", "st", 16,
                            lambda e, nt=nt: e.dma_start(out=yo_t[nt], in_=OUT[:, 0:D]))
            slab_te_end[qs] = p.now("te")

        p.wait("sync", "st")
        p.barrier()
    if standalone:
        p.run()
        _octx.__exit__(None, None, None)
    return nc



CHAIN = [
    ("m_", lambda **kw: build_mamba(**kw)),
    ("f0_", lambda **kw: build_ffn_int8(**kw)),
    ("a1_", lambda **kw: build_attn("l1", **kw)),
    ("f1_", lambda **kw: build_ffn_int8(**kw)),
    ("a2_", lambda **kw: build_l2_attn(**kw)),
    ("f2_", lambda **kw: build_l2_ffn(**kw)),
    ("a3_", lambda **kw: build_attn("l3", **kw)),
    ("f3_", lambda **kw: build_l3_ffn(**kw)),
]


def build_fused(T_=None):
    Tl = T_ or T
    nc = bass.Bass("TRN2", target_bir_lowering=False)
    octx = ExitStack()
    octx.__enter__()
    p = Prog(nc, octx)
    xs_dram = [nc.dram_tensor("x", [Tl, D], F32, kind="ExternalInput").ap()]
    for i in range(len(CHAIN) - 1):
        xs_dram.append(nc.dram_tensor(f"xi{i}", [Tl, D], F32, kind="Internal").ap())
    y_final = nc.dram_tensor("y", [Tl, D], F32, kind="ExternalOutput").ap()

    for k, (pfx, fn) in enumerate(CHAIN):
        p.pfx = pfx
        xin = xs_dram[k]
        yout = xs_dram[k + 1] if k < len(CHAIN) - 1 else y_final

        def dram(name, shape, dt, kind, pfx=pfx, xin=xin, yout=yout):
            if name == "x":
                return xin
            if name == "y":
                return yout
            return nc.dram_tensor(pfx + name, shape, dt, kind="ExternalInput").ap()

        fn(T_=Tl, p=p, dram=dram)
    p.run()
    octx.__exit__(None, None, None)
    return nc



def make_consts():
    return {"c_ident": np.eye(128, dtype=np.float32)}


def _rope_consts(Tl):
    theta = 1.0 / 10000.0 ** (np.arange(0, 64, 2, dtype=np.float32) / 64)
    COSC = np.zeros((128, 8), np.float32)
    SINC = np.zeros((128, 8), np.float32)
    for col in range(8):
        for p_ in range(128):
            r = col * 128 + p_
            h = r // 64
            j = (r % 64) // 2
            ang = np.float32(h) * theta[j]
            COSC[p_, col] = np.cos(ang)
            SINC[p_, col] = (-np.sin(ang)) if (r % 2 == 0) else np.sin(ang)
    perm = np.zeros((128, 128), np.float32)
    for i in range(0, 128, 2):
        perm[i + 1, i] = 1.0
        perm[i, i + 1] = 1.0
    mask = np.triu(np.ones((128, 128), np.float32))
    return dict(c_cosc=COSC, c_sinc=SINC, c_perm=perm, c_mask=mask)


def _l2_consts():
    NH = 8
    gammas = 1.0 - np.exp(np.linspace(np.log(1 / 512), np.log(1 / 32), NH))
    GP = np.zeros((128, NH * 36), np.float32)
    for h in range(NH):
        for mm in range(-3, 33):
            e = 128.0 * mm - np.arange(128)
            GP[:, h * 36 + mm + 3] = np.exp(e * np.log(gammas[h]))
    RAMP = np.zeros((128, NH * 512), np.float32)
    for h in range(NH):
        RAMP[:, h * 512:(h + 1) * 512] = (gammas[h] ** np.arange(512))[None, :]
    return dict(c_gp=GP, c_ramp=RAMP, c_mask=np.triu(np.ones((128, 128), np.float32)))


def _ehbc():
    ehb = np.zeros((32, 16 * 128), np.float32)
    ehc = np.zeros((32, 16 * 128), np.float32)
    for n in range(16):
        ehb[n, n * 128:(n + 1) * 128] = 1.0
        ehc[16 + n, n * 128:(n + 1) * 128] = 1.0
    return dict(c_ehb=ehb, c_ehc=ehc)


def _col8(v):
    return np.ascontiguousarray(v.reshape(8, 128).T)


def _col16(v):
    return np.ascontiguousarray(v.reshape(16, 128).T)


def _ct(a):
    return np.ascontiguousarray(a)



def kernel(**inputs):
    ii = inputs
    consts = make_consts()

    w_mamba = dict(
        lnw=_ct(ii["L0_attn_ln_w"][None]), lnb=_ct(ii["L0_attn_ln_b"][None]),
        wipT=_ct(ii["L0_attn_in_proj_w"].T),
        convw=_ct(ii["L0_attn_conv_w"].reshape(2048, 4).reshape(16, 128, 4)
                  .transpose(1, 0, 2).reshape(128, 64)),
        convb=_col16(ii["L0_attn_conv_b"]),
        wxpT=_ct(ii["L0_attn_x_proj_w"].T),
        wdtT=_ct(ii["L0_attn_dt_proj_w"].T),
        dtb=_col16(ii["L0_attn_dt_proj_b"]),
        alog=_ct(ii["L0_attn_A_log"].reshape(16, 128, 16).transpose(1, 0, 2)
                 .reshape(128, 256)),
        dmat=_col16(ii["L0_attn_D"]),
        wopT=_ct(ii["L0_attn_out_proj_w"].T),
        **_ehbc(), **consts)
    w_ffn0 = dict(wgT=_ct(ii["L0_ffn_w_gate"].T).astype(np.int8),
                  wuT=_ct(ii["L0_ffn_w_up"].T).astype(np.int8),
                  wdT=_ct(ii["L0_ffn_w_down"].T).astype(np.int8), **consts)
    w_attn1 = dict(wqkv=_ct(ii["L1_attn_W_qkv"]), bqkv=_ct(ii["L1_attn_b_qkv"][None]),
                   bqkvc=_ct(ii["L1_attn_b_qkv"].reshape(24, 128).T),
                   wproj=_ct(ii["L1_attn_W_proj"]), bproj=_ct(ii["L1_attn_b_proj"][None]),
                   aqkv=_ct(ii["L1_attn_A_qkv"]), bl_qkv=_ct(ii["L1_attn_B_qkv"]),
                   aproj=_ct(ii["L1_attn_A_proj"]), bl_proj=_ct(ii["L1_attn_B_proj"]),
                   **consts)
    w_ffn1 = dict(wgT=_ct(ii["L1_ffn_w_gate"].T).astype(np.int8),
                  wuT=_ct(ii["L1_ffn_w_up"].T).astype(np.int8),
                  wdT=_ct(ii["L1_ffn_w_down"].T).astype(np.int8), **consts)
    w_attn2 = dict(wq=_ct(ii["L2_attn_Wq"].transpose(1, 0, 2).reshape(128, 8 * 128)),
                   wk=_ct(ii["L2_attn_Wk"].transpose(1, 0, 2).reshape(128, 8 * 128)),
                   wv=_ct(ii["L2_attn_Wv"].transpose(1, 0, 2).reshape(128, 8 * 128)),
                   w1=_ct(ii["L2_attn_W1"]), w2=_ct(ii["L2_attn_W2"]),
                   gnw=_col8(ii["L2_attn_gn_w"]), gnb=_col8(ii["L2_attn_gn_b"]),
                   **_l2_consts(), **consts)
    w_ffn2 = dict(wupT=_ct(ii["L2_ffn_w_up"].T), wdnT=_ct(ii["L2_ffn_w_down"].T), **consts)
    w_attn3 = dict(wqkv=_ct(ii["L3_attn_W_attn"]), bqkv=_ct(ii["L3_attn_b_attn"][None]),
                   bqkvc=_ct(ii["L3_attn_b_attn"].reshape(24, 128).T),
                   wproj=_ct(ii["L3_attn_W_proj"]), bproj=_ct(ii["L3_attn_b_proj"][None]),
                   **_rope_consts(T), **consts)
    w_ffn3 = dict(wrT=_ct(ii["L3_ffn_W_r"].T), wkT=_ct(ii["L3_ffn_W_k"].T),
                  wvT=_ct(ii["L3_ffn_W_v"].T),
                  mur=_col8(ii["L3_ffn_mu_r"]), muk=_col8(ii["L3_ffn_mu_k"]),
                  **consts)

    wmaps = [w_mamba, w_ffn0, w_attn1, w_ffn1, w_attn2, w_ffn2, w_attn3, w_ffn3]
    weights = {}
    for (pfx, _), wm in zip(CHAIN, wmaps):
        for k, v in wm.items():
            weights[pfx + k] = v

    nc = build_fused()
    in_maps = [dict(x=np.ascontiguousarray(ii["L0_attn_x"][b]), **weights) for b in range(8)]
    res = run_bass_kernel_spmd(nc, in_maps, core_ids=list(range(8)))
    return np.stack([np.asarray(res.results[b]["y"]) for b in range(8)], axis=0)
